# version 01
# baseline (speedup 1.0000x reference)
import numpy as np
from contextlib import ExitStack

import concourse.bass as bass
import concourse.mybir as mybir
from concourse.bass_utils import run_bass_kernel_spmd
from concourse.bass_core_rust import IndirectOffsetOnAxis

F32 = mybir.dt.float32
BF16 = mybir.dt.bfloat16
I32 = mybir.dt.int32
U16 = mybir.dt.uint16
U32 = mybir.dt.uint32
AF = mybir.ActivationFunctionType
ALU = mybir.AluOpType
AX = mybir.AxisListType

NT = 16
NPT = 16
NSLOT = 64
SGRP = 4
NGRP = NSLOT // SGRP


class Sched:
    ENGINES = ("sync", "vector", "scalar", "tensor", "gpsimd")

    def __init__(self):
        self.ops = {e: [] for e in self.ENGINES}
        self.cnt = {}

    def inc(self, sem, by):
        self.cnt[sem] = self.cnt.get(sem, 0) + by
        return self.cnt[sem]

    def get(self, sem):
        return self.cnt.get(sem, 0)

    def on(self, eng, fn):
        self.ops[eng].append(fn)


def build_program(dbg=False, phase="full"):
    nc = bass.Bass("TRN2", target_bir_lowering=False)

    x_own = nc.dram_tensor("x_own", [2048, 1024], F32, kind="ExternalInput")
    x_pre = nc.dram_tensor("x_pre", [2048, 1024], F32, kind="ExternalInput")
    wpT = nc.dram_tensor("wpT", [1024, 1024], F32, kind="ExternalInput")
    wmT = nc.dram_tensor("wmT", [1024, 1024], F32, kind="ExternalInput")
    wqT = nc.dram_tensor("wqT", [1024, 2048], F32, kind="ExternalInput")
    keysT = nc.dram_tensor("keysT", [16, 128, 256], F32, kind="ExternalInput")
    pmT = nc.dram_tensor("pmT", [16, 64, 64], F32, kind="ExternalInput")
    trf = nc.dram_tensor("trf", [16, 64, 64], F32, kind="ExternalInput")
    nw_in = nc.dram_tensor("nw_in", [1, 1024], F32, kind="ExternalInput")
    comb = nc.dram_tensor("comb", [65536, 2048], F32, kind="ExternalInput")
    out = nc.dram_tensor("out", [2048, 1024], F32, kind="ExternalOutput")
    dbg_t = {}
    if dbg:
        for nm, sh in (("y1d", [2048, 1024]), ("fid", [2048, 64]),
                       ("fsd", [2048, 64]), ("hrd", [2048, 64]),
                       ("xnd", [2048, 1024]), ("pbd", [2048, 1024])):
            dt_ = {"fid": U32, "xnd": BF16, "pbd": BF16}.get(nm, F32)
            dbg_t[nm] = nc.dram_tensor(nm, sh, dt_, kind="ExternalOutput")
        dbg_t["simd"] = nc.dram_tensor("simd", [2048, 512], F32, kind="ExternalOutput")
        dbg_t["csd"] = nc.dram_tensor("csd", [2048, 16], F32, kind="ExternalOutput")
        dbg_t["cod"] = nc.dram_tensor("cod", [128, 192], F32, kind="ExternalOutput")
        dbg_t["cud"] = nc.dram_tensor("cud", [128, 24], U16, kind="ExternalOutput")
        dbg_t["cid"] = nc.dram_tensor("cid", [128, 24], I32, kind="ExternalOutput")
        dbg_t["nwd"] = nc.dram_tensor("nwd", [128, 1024], F32, kind="ExternalOutput")
        dbg_t["qtd"] = nc.dram_tensor("qtd", [2048, 256], BF16, kind="ExternalOutput")

    with ExitStack() as ctx:
        E = ctx.enter_context
        WPT = E(nc.sbuf_tensor("WPT", [128, 8192], BF16))
        WQT = E(nc.sbuf_tensor("WQT", [128, 16384], BF16))
        WMT = E(nc.sbuf_tensor("WMT", [128, 8192], BF16))
        KEYS = E(nc.sbuf_tensor("KEYS", [128, 4096], BF16))
        PMTS = E(nc.sbuf_tensor("PMTS", [128, 1024], BF16))
        MSB = E(nc.sbuf_tensor("MSB", [128, 1024], BF16))
        TRFS = E(nc.sbuf_tensor("TRFS", [128, 1024], BF16))
        NWR = E(nc.sbuf_tensor("NWR", [128, 1024], F32))
        IOTAF = E(nc.sbuf_tensor("IOTAF", [128, 128], F32))
        ROWID = E(nc.sbuf_tensor("ROWID", [128, 1], F32))
        IOTA8 = E(nc.sbuf_tensor("IOTA8", [128, 8], F32))
        I128 = E(nc.sbuf_tensor("I128", [128, 128], BF16))
        TRIL = E(nc.sbuf_tensor("TRIL", [128, 128], BF16))

        XF32 = [E(nc.sbuf_tensor(f"XF32_{i}", [128, 1024], F32)) for i in range(2)]
        XB = [E(nc.sbuf_tensor(f"XB_{i}", [128, 1024], BF16)) for i in range(2)]
        XTC = [E(nc.sbuf_tensor(f"XTC_{i}", [128, 1024], BF16)) for i in range(2)]
        PB = [E(nc.sbuf_tensor(f"PB_{i}", [128, 1024], BF16)) for i in range(2)]
        PTC = [E(nc.sbuf_tensor(f"PTC_{i}", [128, 2048], BF16)) for i in range(2)]
        QMSB = E(nc.sbuf_tensor("QMSB", [128, 256], BF16))
        SSB = E(nc.sbuf_tensor("SSB", [128, 1024], BF16))
        AMSK = E(nc.sbuf_tensor("AMSK", [128, 256], BF16))
        OTSB = E(nc.sbuf_tensor("OTSB", [128, 256], BF16))
        NUD = E(nc.sbuf_tensor("NUD", [128, 1024], BF16))
        U1 = E(nc.sbuf_tensor("U1", [128, 1024], F32))
        U2 = E(nc.sbuf_tensor("U2", [128, 1024], F32))
        Y1 = [E(nc.sbuf_tensor(f"Y1_{i}", [128, 1024], F32)) for i in range(3)]
        XNB = [E(nc.sbuf_tensor(f"XNB_{i}", [128, 1024], BF16)) for i in range(3)]
        XNT = [E(nc.sbuf_tensor(f"XNT_{i}", [128, 1024], BF16)) for i in range(2)]
        QTT = [E(nc.sbuf_tensor("QTT_0", [128, 2048], BF16))]
        SIMS = E(nc.sbuf_tensor("SIMS", [128, 512], F32))
        COLS = E(nc.sbuf_tensor("COLS", [128, 192], F32))
        COLI = E(nc.sbuf_tensor("COLI", [128, 24], I32))
        COLU = E(nc.sbuf_tensor("COLU", [128, 24], U16))
        OH = E(nc.sbuf_tensor("OH", [128, 64], F32))
        FI = [E(nc.sbuf_tensor(f"FI_{i}", [128, 64], U32)) for i in range(2)]
        FSR = [E(nc.sbuf_tensor(f"FSR_{i}", [128, 64], F32)) for i in range(2)]
        HRAW = [E(nc.sbuf_tensor(f"HRAW_{i}", [128, 64], F32)) for i in range(2)]
        GH = [E(nc.sbuf_tensor(f"GH_{i}", [128, 64], F32)) for i in range(2)]
        GAT = [E(nc.sbuf_tensor(f"GAT_{i}", [128, SGRP * 2048], BF16))
               for i in range(2)]
        SBL = [E(nc.sbuf_tensor(f"SBL_{i}", [128, SGRP * 128], BF16))
               for i in range(2)]
        I1A = E(nc.sbuf_tensor("I1A", [128, 64], F32))
        I2A = E(nc.sbuf_tensor("I2A", [128, 64], F32))
        PKA = E(nc.sbuf_tensor("PKA", [128, 64], F32))
        FSA = E(nc.sbuf_tensor("FSA", [128, 64], F32))
        OHW = E(nc.sbuf_tensor("OHW", [128, 512], F32))
        I1S = E(nc.sbuf_tensor("I1S", [128, 64], F32))
        I2S = E(nc.sbuf_tensor("I2S", [128, 64], F32))
        Y2 = [E(nc.sbuf_tensor(f"Y2_{i}", [128, 1024], F32)) for i in range(2)]
        CS = E(nc.sbuf_tensor("CS", [128, 16], F32))

        ps_mm = E(nc.psum_tensor("ps_mm", [128, 1024], F32))
        ps_up = E(nc.psum_tensor("ps_up", [128, 1024], F32))
        ps_state = E(nc.psum_tensor("ps_state", [64, 1024], F32))
        ps_scan = E(nc.psum_tensor("ps_scan", [128, 512], F32))
        ps_tr = E(nc.psum_tensor("ps_tr", [128, 1024], BF16))

        sem_names = ["LX", "LW", "G", "OUT", "PE", "VE", "SC", "GP"]
        SEM = {n: E(nc.semaphore(n)) for n in sem_names}
        S = Sched()

        def W(eng, sem, val):
            if val > 0:
                eng.wait_ge(SEM[sem], val)

        def ph_setup():
            def g_load(gp):
                gp.dma_start(out=WPT[:].rearrange("p (dc j) -> p dc j", j=1024),
                             in_=wpT[:].rearrange("(dc p) j -> p dc j", p=128)).then_inc(SEM["LW"], 16)
                gp.dma_start(out=WMT[:].rearrange("p (dc j) -> p dc j", j=1024),
                             in_=wmT[:].rearrange("(dc p) j -> p dc j", p=128)).then_inc(SEM["LW"], 16)
                gp.dma_start(out=WQT[:].rearrange("p (dc j) -> p dc j", j=2048),
                             in_=wqT[:].rearrange("(dc p) j -> p dc j", p=128)).then_inc(SEM["LW"], 16)
                gp.dma_start(out=KEYS[:].rearrange("p (c n) -> p c n", n=256),
                             in_=keysT[:].rearrange("c p n -> p c n")).then_inc(SEM["LW"], 16)
                gp.dma_start(out=PMTS[0:64, :].rearrange("p (h i) -> p h i", i=64),
                             in_=pmT[:].rearrange("h p i -> p h i")).then_inc(SEM["LW"], 16)
                gp.dma_start(out=TRFS[0:64, :].rearrange("p (h j) -> p h j", j=64),
                             in_=trf[:].rearrange("h p j -> p h j")).then_inc(SEM["LW"], 16)
                gp.dma_start(out=NWR[:],
                             in_=bass.AP(nw_in, 0, [[0, 128], [1, 1024]])).then_inc(SEM["LW"], 16)
                gp.iota(IOTAF[:], [[1, 128]], channel_multiplier=0,
                        allow_small_or_imprecise_dtypes=True)
                gp.iota(ROWID[:], [[0, 1]], channel_multiplier=1,
                        allow_small_or_imprecise_dtypes=True)
                gp.iota(IOTA8[:], [[1, 8]], channel_multiplier=0,
                        allow_small_or_imprecise_dtypes=True).then_inc(SEM["GP"], 1)
            S.on("gpsimd", g_load)
            lw_cnt = S.inc("LW", 16 * 7)
            gp_cnt = S.inc("GP", 1)

            def v_consts(ve):
                W(ve, "GP", gp_cnt)
                ve.tensor_scalar(I128[:], IOTAF[:], ROWID[:, 0:1], None, op0=ALU.is_equal)
                ve.tensor_scalar(TRIL[:], IOTAF[:], ROWID[:, 0:1], None, op0=ALU.is_ge)
                ve.memset(CS[:, 12:13], 1e-5)
                ve.memset(CS[:, 13:14], 1e-6).then_inc(SEM["VE"], 1)
            S.on("vector", v_consts)
            ve_c = S.inc("VE", 1)

            def t_setup(te):
                W(te, "LW", lw_cnt)
                W(te, "VE", ve_c)
                for h in range(16):
                    te.matmul(ps_state[0:64, h * 64:(h + 1) * 64],
                              PMTS[0:64, h * 64:(h + 1) * 64],
                              PMTS[0:64, h * 64:(h + 1) * 64],
                              start=True, stop=True)
                te.matmul(ps_scan[0:64, 0:64], PMTS[0:64, 0:64], PMTS[0:64, 0:64],
                          start=True, stop=True).then_inc(SEM["PE"], 1)
            S.on("tensor", t_setup)
            pe_c = S.inc("PE", 1)

            def v_setup2(ve):
                W(ve, "PE", pe_c)
                ve.tensor_copy(MSB[0:64, :], ps_state[0:64, :]).then_inc(SEM["VE"], 1)
            S.on("vector", v_setup2)
            return S.inc("VE", 1)

        def ph_xload(src, ti, par, wait_ve_xbuf, wait_ve_psmm, wait_sc_pstr=0):
            def ld(sy):
                W(sy, "VE", wait_ve_xbuf)
                sy.dma_start(out=XF32[par][:],
                             in_=src[ti * 128:(ti + 1) * 128, :]).then_inc(SEM["LX"], 16)
            S.on("sync", ld)
            lx = S.inc("LX", 16)

            def cast(ve):
                W(ve, "LX", lx)
                ve.tensor_copy(XB[par][:], XF32[par][:]).then_inc(SEM["VE"], 1)
            S.on("vector", cast)
            ve_cast = S.inc("VE", 1)

            def tr(te):
                W(te, "VE", ve_cast)
                if wait_sc_pstr > 0:
                    W(te, "SC", wait_sc_pstr)
                li = None
                for dc in range(8):
                    li = te.transpose(ps_tr[:, dc * 128:(dc + 1) * 128],
                                      XB[par][:, dc * 128:(dc + 1) * 128], I128[:])
                li.then_inc(SEM["PE"], 1)
            S.on("tensor", tr)
            pe_tr = S.inc("PE", 1)

            def ev(sc):
                W(sc, "PE", pe_tr)
                sc.copy(XTC[par][:], ps_tr[:]).then_inc(SEM["SC"], 1)
            S.on("scalar", ev)
            sc_ev = S.inc("SC", 1)

            def proj(te):
                W(te, "SC", sc_ev)
                W(te, "VE", wait_ve_psmm)
                li = None
                for half in range(2):
                    for dc in range(8):
                        li = te.matmul(
                            ps_mm[:, half * 512:(half + 1) * 512],
                            XTC[par][:, dc * 128:(dc + 1) * 128],
                            WPT[:, dc * 1024 + half * 512: dc * 1024 + (half + 1) * 512],
                            start=(dc == 0), stop=(dc == 7))
                li.then_inc(SEM["PE"], 1)
            S.on("tensor", proj)
            pe_proj = S.inc("PE", 1)

            def pb(ve):
                W(ve, "PE", pe_proj)
                ve.tensor_copy(PB[par][:], ps_mm[:]).then_inc(SEM["VE"], 1)
            S.on("vector", pb)
            ve_pb = S.inc("VE", 1)
            return ve_pb

        def ph_state_update(par, wait_ve, first):
            def st(te):
                W(te, "VE", wait_ve)
                li = None
                for h in range(16):
                    li = te.matmul(ps_state[0:64, h * 64:(h + 1) * 64],
                                   PB[par][:, h * 64:(h + 1) * 64],
                                   PB[par][:, h * 64:(h + 1) * 64],
                                   start=(first and h in (0, 8)), stop=False,
                                   skip_group_check=True)
                li.then_inc(SEM["PE"], 1)
            S.on("tensor", st)
            return S.inc("PE", 1)

        def ph_scan_tile(ti, par, ve_pb):
            def ptr_a(te):
                W(te, "VE", ve_pb)
                li = None
                for h in range(8):
                    li = te.transpose(ps_tr[0:64, h * 128:(h + 1) * 128],
                                      PB[par][:, h * 64:(h + 1) * 64], I128[:])
                li.then_inc(SEM["PE"], 1)
            S.on("tensor", ptr_a)
            pe_ptr_a = S.inc("PE", 1)

            def ptev_a(sc):
                W(sc, "PE", pe_ptr_a)
                sc.copy(PTC[par][0:64, 0:1024], ps_tr[0:64, 0:1024]).then_inc(SEM["SC"], 1)
            S.on("scalar", ptev_a)
            sc_pt_a = S.inc("SC", 1)

            def ptr_b(te):
                W(te, "SC", sc_pt_a)
                li = None
                for h in range(8, 16):
                    li = te.transpose(ps_tr[0:64, (h - 8) * 128:(h - 7) * 128],
                                      PB[par][:, h * 64:(h + 1) * 64], I128[:])
                li.then_inc(SEM["PE"], 1)
            S.on("tensor", ptr_b)
            pe_ptr_b = S.inc("PE", 1)

            def ptev_b(sc):
                W(sc, "PE", pe_ptr_b)
                sc.copy(PTC[par][0:64, 1024:2048], ps_tr[0:64, 0:1024]).then_inc(SEM["SC"], 1)
            S.on("scalar", ptev_b)
            sc_pt = S.inc("SC", 1)

            def ssnap(ve):
                ve.tensor_copy(SSB[0:64, :], ps_state[0:64, :]).then_inc(SEM["VE"], 1)
            S.on("vector", ssnap)
            ve_snap = S.inc("VE", 1)

            last_ve = ve_snap
            ve_e3 = 0
            for h in range(16):
                hp = h % 2

                def s1(te, h=h, first=(h == 0), sc_pt=sc_pt, ve_snap=ve_snap):
                    if first:
                        W(te, "SC", sc_pt)
                        W(te, "VE", ve_snap)
                    te.matmul(ps_scan[0:64, 128:256],
                              MSB[0:64, h * 64:(h + 1) * 64],
                              PTC[par][0:64, h * 128:(h + 1) * 128],
                              start=True, stop=True).then_inc(SEM["PE"], 1)
                S.on("tensor", s1)
                pe_s1 = S.inc("PE", 1)

                def e1(ve, h=h, pe_s1=pe_s1):
                    W(ve, "PE", pe_s1)
                    ve.tensor_copy(QMSB[0:64, (h % 2) * 128:(h % 2 + 1) * 128],
                                   ps_scan[0:64, 128:256]).then_inc(SEM["VE"], 1)
                S.on("vector", e1)
                ve_e1 = S.inc("VE", 1)

                def s2(te, h=h, ve_e1=ve_e1):
                    W(te, "VE", ve_e1)
                    te.matmul(ps_scan[:, 0:128],
                              PTC[par][0:64, h * 128:(h + 1) * 128],
                              QMSB[0:64, (h % 2) * 128:(h % 2 + 1) * 128],
                              start=True, stop=True).then_inc(SEM["PE"], 1)
                S.on("tensor", s2)
                pe_s2 = S.inc("PE", 1)

                def msk(ve, hp=hp, pe_s2=pe_s2):
                    W(ve, "PE", pe_s2)
                    ve.tensor_mul(AMSK[:, hp * 128:(hp + 1) * 128],
                                  ps_scan[:, 0:128], TRIL[:]).then_inc(SEM["VE"], 1)
                S.on("vector", msk)
                ve_msk = S.inc("VE", 1)

                def s34(te, h=h, hp=hp, ve_msk=ve_msk):
                    W(te, "VE", ve_msk)
                    te.matmul(ps_scan[0:64, 256:384],
                              SSB[0:64, h * 64:(h + 1) * 64],
                              QMSB[0:64, (h % 2) * 128:(h % 2 + 1) * 128],
                              start=True, stop=False, skip_group_check=True)
                    te.matmul(ps_scan[0:64, 256:384],
                              PB[par][:, h * 64:(h + 1) * 64],
                              AMSK[:, hp * 128:(hp + 1) * 128],
                              start=False, stop=True,
                              skip_group_check=True).then_inc(SEM["PE"], 1)
                S.on("tensor", s34)
                pe_s34 = S.inc("PE", 1)

                def e2(sc, h=h, pe_s34=pe_s34):
                    W(sc, "PE", pe_s34)
                    sc.mul(OTSB[0:64, (h % 2) * 128:(h % 2 + 1) * 128],
                           ps_scan[0:64, 256:384], 0.125).then_inc(SEM["SC"], 1)
                S.on("scalar", e2)
                sc_e2 = S.inc("SC", 1)

                def s5(te, h=h, sc_e2=sc_e2, ve_e3=ve_e3):
                    W(te, "SC", sc_e2)
                    if h % 2 == 0 and ve_e3 > 0:
                        W(te, "VE", ve_e3)
                    tp = (0, 64) if h % 2 == 1 else (0, 0)
                    dst = (ps_scan[64:128, 384:512] if h % 2 == 1
                           else ps_scan[0:64, 384:512])
                    te.matmul(dst, TRFS[0:64, h * 64:(h + 1) * 64],
                              OTSB[0:64, (h % 2) * 128:(h % 2 + 1) * 128],
                              start=True, stop=True,
                              tile_position=tp).then_inc(SEM["PE"], 1)
                S.on("tensor", s5)
                pe_s5 = S.inc("PE", 1)

                if h % 2 == 1:
                    def e3(ve, h=h, pe_s5=pe_s5):
                        W(ve, "PE", pe_s5)
                        ve.tensor_copy(NUD[:, (h // 2) * 128:(h // 2 + 1) * 128],
                                       ps_scan[:, 384:512]).then_inc(SEM["VE"], 1)
                    S.on("vector", e3)
                    ve_e3 = S.inc("VE", 1)
                    last_ve = ve_e3
            return last_ve

        def ph_mixer_ln1(ti, par, ve_nud, y1buf, xnbuf, xntbuf):
            def mx(te):
                W(te, "VE", ve_nud)
                li = None
                for half in range(2):
                    for c in range(8):
                        li = te.matmul(
                            ps_mm[:, half * 512:(half + 1) * 512],
                            NUD[:, c * 128:(c + 1) * 128],
                            WMT[:, c * 1024 + half * 512: c * 1024 + (half + 1) * 512],
                            start=(c == 0), stop=(c == 7))
                li.then_inc(SEM["PE"], 1)
            S.on("tensor", mx)
            pe_mx = S.inc("PE", 1)

            def ln1(ve):
                W(ve, "PE", pe_mx)
                ve.tensor_add(U1[:], XF32[par][:], ps_mm[:])
                ve.reduce_sum(CS[:, 0:1], U1[:], axis=AX.X)
                ve.tensor_mul(U2[:], U1[:], U1[:])
                ve.reduce_sum(CS[:, 1:2], U2[:], axis=AX.X)
                ve.tensor_scalar(CS[:, 2:3], CS[:, 0:1], 1.0 / 1024, None, op0=ALU.mult)
                ve.tensor_scalar(CS[:, 3:4], CS[:, 1:2], 1.0 / 1024, None, op0=ALU.mult)
                ve.tensor_mul(CS[:, 4:5], CS[:, 2:3], CS[:, 2:3])
                ve.tensor_copy(CS[:, 15:16], CS[:, 0:1])
                ve.tensor_sub(CS[:, 5:6], CS[:, 3:4], CS[:, 4:5]).then_inc(SEM["VE"], 1)
            S.on("vector", ln1)
            ve_st = S.inc("VE", 1)

            def rst(sc):
                W(sc, "VE", ve_st)
                sc.activation(out=CS[:, 7:8], in_=CS[:, 5:6], func=AF.Sqrt,
                              bias=CS[:, 12:13], scale=1.0).then_inc(SEM["SC"], 1)
            S.on("scalar", rst)
            sc_rst = S.inc("SC", 1)

            def ln1b(ve):
                W(ve, "SC", sc_rst)
                ve.reciprocal(CS[:, 6:7], CS[:, 7:8])
                ve.tensor_scalar(y1buf[:], U1[:], CS[:, 2:3], CS[:, 6:7],
                                 op0=ALU.subtract, op1=ALU.mult)
                ve.tensor_mul(U2[:], y1buf[:], y1buf[:])
                ve.reduce_sum(CS[:, 8:9], U2[:], axis=AX.X)
                ve.tensor_copy(CS[:, 15:16], CS[:, 0:1])
                ve.tensor_scalar(CS[:, 9:10], CS[:, 8:9], 1.0 / 1024, None,
                                 op0=ALU.mult).then_inc(SEM["VE"], 1)
            S.on("vector", ln1b)
            ve_ms = S.inc("VE", 1)

            def rin(sc):
                W(sc, "VE", ve_ms)
                sc.activation(out=CS[:, 11:12], in_=CS[:, 9:10], func=AF.Sqrt,
                              bias=CS[:, 13:14], scale=1.0).then_inc(SEM["SC"], 1)
            S.on("scalar", rin)
            sc_rin = S.inc("SC", 1)

            def xn(ve):
                W(ve, "SC", sc_rin)
                ve.reciprocal(CS[:, 10:11], CS[:, 11:12])
                ve.tensor_copy(CS[:, 15:16], CS[:, 0:1])
                ve.tensor_scalar(U2[:], y1buf[:], CS[:, 10:11], None, op0=ALU.mult)
                ve.tensor_mul(xnbuf[:], U2[:], NWR[:]).then_inc(SEM["VE"], 1)
            S.on("vector", xn)
            ve_xn = S.inc("VE", 1)

            if dbg and ti == 3:
                def xdump(sy, xnbuf=xnbuf, y1buf=y1buf, ve_xn=ve_xn):
                    W(sy, "VE", ve_xn)
                    sy.dma_start(out=dbg_t["xnd"][0:128, :], in_=xnbuf[:]).then_inc(SEM["OUT"], 16)
                    sy.dma_start(out=dbg_t["y1d"][0:128, :], in_=y1buf[:]).then_inc(SEM["OUT"], 16)
                    sy.dma_start(out=dbg_t["csd"][0:128, :], in_=CS[:]).then_inc(SEM["OUT"], 16)
                    sy.dma_start(out=dbg_t["nwd"][:], in_=NWR[:]).then_inc(SEM["OUT"], 16)
                S.on("sync", xdump)
                dump_out = S.inc("OUT", 64)
            else:
                dump_out = 0

            def xtr(te):
                if dump_out > 0:
                    W(te, "OUT", dump_out)
                W(te, "VE", ve_xn)
                li = None
                for dc in range(8):
                    li = te.transpose(ps_tr[:, dc * 128:(dc + 1) * 128],
                                      xnbuf[:, dc * 128:(dc + 1) * 128], I128[:])
                li.then_inc(SEM["PE"], 1)
            S.on("tensor", xtr)
            pe_xtr = S.inc("PE", 1)

            def xev(sc):
                W(sc, "PE", pe_xtr)
                sc.copy(xntbuf[:], ps_tr[:]).then_inc(SEM["SC"], 1)
            S.on("scalar", xev)
            sc_xev = S.inc("SC", 1)
            return ve_xn, sc_xev

        def ph_q_sims_routing(ti, par, sc_xnt, xntbuf, fibuf, fsrbuf):
            qt = QTT[0]
            ve_qprev = 0
            for grp in range(2):
                def qmm(te, grp=grp, sc_xnt=sc_xnt, ve_qprev=ve_qprev):
                    if grp == 0:
                        W(te, "SC", sc_xnt)
                    else:
                        W(te, "VE", ve_qprev)
                    li = None
                    for qs in range(8):
                        qq = grp * 8 + qs
                        for dc in range(8):
                            li = te.matmul(
                                ps_mm[:, qs * 128:(qs + 1) * 128],
                                WQT[:, dc * 2048 + qq * 128: dc * 2048 + (qq + 1) * 128],
                                xntbuf[:, dc * 128:(dc + 1) * 128],
                                start=(dc == 0), stop=(dc == 7))
                    li.then_inc(SEM["PE"], 1)
                S.on("tensor", qmm)
                pe_q = S.inc("PE", 1)

                def qev(ve, grp=grp, pe_q=pe_q):
                    W(ve, "PE", pe_q)
                    ve.tensor_copy(qt[:, grp * 1024:(grp + 1) * 1024],
                                   ps_mm[:]).then_inc(SEM["VE"], 1)
                S.on("vector", qev)
                ve_qprev = S.inc("VE", 1)
            ve_qdone = ve_qprev

            ve_last = ve_qdone
            ve_sev_prev = 0
            for h in range(8):
                def smm(te, h=h, ve_qdone=ve_qdone, ve_sev_prev=ve_sev_prev):
                    if h == 0:
                        W(te, "VE", ve_qdone)
                    else:
                        W(te, "VE", ve_sev_prev)
                    li = None
                    for which in range(2):
                        qs = h * 2 + which
                        li = te.matmul(ps_mm[:, which * 256:(which + 1) * 256],
                                       qt[:, qs * 128:(qs + 1) * 128],
                                       KEYS[:, qs * 256:(qs + 1) * 256],
                                       start=True, stop=True)
                    li.then_inc(SEM["PE"], 1)
                S.on("tensor", smm)
                pe_sm = S.inc("PE", 1)

                def sev(ve, pe_sm=pe_sm):
                    W(ve, "PE", pe_sm)
                    ve.tensor_copy(SIMS[:], ps_mm[:, 0:512]).then_inc(SEM["VE"], 1)
                S.on("vector", sev)
                ve_sev_prev = S.inc("VE", 1)
                sdump_out = 0
                if dbg and h == 0 and ti == 3:
                    def sdump(sy, ve_sev=ve_sev_prev):
                        W(sy, "VE", ve_sev)
                        sy.dma_start(out=dbg_t["simd"][0:128, :],
                                     in_=SIMS[:]).then_inc(SEM["OUT"], 16)
                        sy.dma_start(out=dbg_t["qtd"][0:128, :],
                                     in_=QTT[0][:, 0:256]).then_inc(SEM["OUT"], 16)
                    S.on("sync", sdump)
                    sdump_out = S.inc("OUT", 32)

                def route(ve, h=h, sdump_out=sdump_out):
                    if sdump_out > 0:
                        W(ve, "OUT", sdump_out)
                    s1t, s2t = COLS[:, 0:8], COLS[:, 8:16]
                    i1u, i2u, pku = COLU[:, 0:8], COLU[:, 8:16], COLU[:, 16:24]
                    ps64 = COLS[:, 96:160]
                    ve.max(out=s1t, in_=SIMS[:, 0:256])
                    ve.tensor_copy(COLS[:, 80:88], s1t)
                    ve.max_index(out=i1u, in_max=s1t, in_values=SIMS[:, 0:256])
                    ve.max(out=s2t, in_=SIMS[:, 256:512])
                    ve.tensor_copy(COLS[:, 80:88], s2t)
                    ve.max_index(out=i2u, in_max=s2t, in_values=SIMS[:, 256:512])
                    ve.tensor_tensor(ps64.rearrange("p (a b) -> p a b", b=8),
                                     s1t.unsqueeze(2).to_broadcast([128, 8, 8]),
                                     s2t.unsqueeze(1).to_broadcast([128, 8, 8]),
                                     op=ALU.add)
                    ve.max(out=FSA[:, h * 8:(h + 1) * 8], in_=ps64)
                    ve.tensor_copy(COLS[:, 80:88], FSA[:, h * 8:(h + 1) * 8])
                    ve.max_index(out=pku, in_max=FSA[:, h * 8:(h + 1) * 8],
                                 in_values=ps64)
                    ve.tensor_copy(I1A[:, h * 8:(h + 1) * 8], i1u)
                    ve.tensor_copy(I2A[:, h * 8:(h + 1) * 8], i2u)
                    ve.tensor_copy(PKA[:, h * 8:(h + 1) * 8],
                                   pku).then_inc(SEM["VE"], 1)
                S.on("vector", route)
                ve_last = S.inc("VE", 1)
                if dbg and h == 0 and ti == 3:
                    def rdump(sy, ve_r=ve_last):
                        W(sy, "VE", ve_r)
                        sy.dma_start(out=dbg_t["cod"][:], in_=COLS[:]).then_inc(SEM["OUT"], 16)
                        sy.dma_start(out=dbg_t["cud"][:], in_=COLU[:]).then_inc(SEM["OUT"], 16)
                        sy.dma_start(out=dbg_t["cid"][:], in_=COLI[:]).then_inc(SEM["OUT"], 16)
                    S.on("sync", rdump)
                    rdump_out = S.inc("OUT", 48)
                else:
                    rdump_out = 0
                if rdump_out:
                    def rgate(ve, rdump_out=rdump_out):
                        W(ve, "OUT", rdump_out)
                    S.on("vector", rgate)

            def wtail(ve, fibuf=fibuf, fsrbuf=fsrbuf):
                C = 12582912.0
                oh4 = OHW[:].rearrange("p (h j c) -> p h j c", j=8, c=8)
                io4 = IOTA8[:].unsqueeze(1).unsqueeze(2).to_broadcast([128, 8, 8, 8])
                af4 = I1S[:].rearrange("p (h j) -> p h j", j=8).unsqueeze(3)\
                          .to_broadcast([128, 8, 8, 8])
                bf4 = I2S[:].rearrange("p (h j) -> p h j", j=8).unsqueeze(3)\
                          .to_broadcast([128, 8, 8, 8])
                i1a4 = I1A[:].rearrange("p (h c) -> p h c", c=8).unsqueeze(2)\
                           .to_broadcast([128, 8, 8, 8])
                i2a4 = I2A[:].rearrange("p (h c) -> p h c", c=8).unsqueeze(2)\
                           .to_broadcast([128, 8, 8, 8])
                i1sel = COLS[:, 96:160]
                i2sel = COLS[:, 0:64]
                fif = OHW[:, 0:64]
                T1 = OHW[:, 64:128]
                T2 = OHW[:, 128:192]
                SP = OHW[:, 448:512]
                ve.tensor_scalar(T1, PKA[:], -3.5, 0.125, op0=ALU.add, op1=ALU.mult)
                ve.tensor_scalar(fsrbuf[:], FSA[:], 0.0, None, op0=ALU.max)
                ve.tensor_scalar(T2, T1, C, None, op0=ALU.add)
                ve.tensor_copy(SP, PKA[:])
                ve.tensor_scalar(I1S[:], T2, -C, None, op0=ALU.add)
                ve.tensor_copy(SP, PKA[:])
                ve.tensor_copy(SP, PKA[:])
                ve.scalar_tensor_tensor(out=I2S[:], in0=I1S[:], scalar=-8.0,
                                        in1=PKA[:], op0=ALU.mult, op1=ALU.add)
                ve.tensor_tensor(oh4, af4, io4, op=ALU.is_equal)
                ve.tensor_mul(oh4, oh4, i1a4)
                ve.reduce_sum(i1sel.rearrange("p (h j) -> p h j", j=8), oh4, axis=AX.X)
                ve.tensor_tensor(oh4, bf4, io4, op=ALU.is_equal)
                ve.tensor_mul(oh4, oh4, i2a4)
                ve.reduce_sum(i2sel.rearrange("p (h j) -> p h j", j=8), oh4, axis=AX.X)
                ve.tensor_copy(SP, PKA[:])
                ve.tensor_copy(SP, PKA[:])
                ve.scalar_tensor_tensor(out=fif, in0=i1sel, scalar=256.0,
                                        in1=i2sel, op0=ALU.mult, op1=ALU.add)
                ve.tensor_copy(SP, PKA[:])
                ve.tensor_copy(SP, PKA[:])
                ve.tensor_copy(fibuf[:], fif).then_inc(SEM["VE"], 1)
            S.on("vector", wtail)
            ve_last = S.inc("VE", 1)
            if dbg and ti == 3:
                def wdump(sy, ve_last=ve_last):
                    W(sy, "VE", ve_last)
                    sy.dma_start(out=dbg_t["simd"][128:256, 0:64], in_=I1A[:]).then_inc(SEM["OUT"], 16)
                    sy.dma_start(out=dbg_t["simd"][128:256, 64:128], in_=I2A[:]).then_inc(SEM["OUT"], 16)
                    sy.dma_start(out=dbg_t["simd"][128:256, 128:192], in_=PKA[:]).then_inc(SEM["OUT"], 16)
                    sy.dma_start(out=dbg_t["simd"][128:256, 192:256], in_=I1S[:]).then_inc(SEM["OUT"], 16)
                    sy.dma_start(out=dbg_t["simd"][128:256, 256:320], in_=I2S[:]).then_inc(SEM["OUT"], 16)
                    sy.dma_start(out=dbg_t["simd"][128:256, 320:384], in_=FSA[:]).then_inc(SEM["OUT"], 16)
                    sy.dma_start(out=dbg_t["cod"][:], in_=COLS[:]).then_inc(SEM["OUT"], 16)
                S.on("sync", wdump)
                wd_out = S.inc("OUT", 16 * 7)
                def wgate(ve, wd_out=wd_out):
                    W(ve, "OUT", wd_out)
                S.on("vector", wgate)
            return ve_last, ve_sev_prev

        def ph_ffn_tile(ti, par, ve_route, xnbuf, y1buf, fibuf, fsrbuf, y2buf,
                        gat_free_pe, ve_psup_free, out_prev):
            hraw, gh = HRAW[par], GH[par]
            pe_up = []
            for g in range(NGRP):
                gb = GAT[g % 2]
                gfree = gat_free_pe[g % 2] if g < 2 else pe_up[g - 2]
                g_first = []
                for k in range(SGRP):
                    def gath(gp, g=g, k=k, gb=gb, gfree=gfree, ve_route=ve_route):
                        if g == 0 and k == 0:
                            W(gp, "VE", ve_route)
                        if k == 0 and gfree > 0:
                            W(gp, "PE", gfree)
                        j = g * SGRP + k
                        gp.indirect_dma_start(
                            out=gb[:, k * 2048:(k + 1) * 2048], out_offset=None,
                            in_=comb[:],
                            in_offset=IndirectOffsetOnAxis(fibuf[:, j:j + 1], 0),
                        ).then_inc(SEM["G"], 16)
                    S.on("gpsimd", gath)
                    S.inc("G", 16)
                g_cnt = S.get("G")

                def down(ve, g=g, gb=gb, g_cnt=g_cnt, xnbuf=xnbuf, hraw=hraw):
                    W(ve, "G", g_cnt)
                    li = None
                    for k in range(SGRP):
                        j = g * SGRP + k
                        ve.tensor_mul(U1[:], gb[:, k * 2048:k * 2048 + 1024],
                                      xnbuf[:])
                        li = ve.reduce_sum(hraw[:, j:j + 1], U1[:], axis=AX.X)
                    li.then_inc(SEM["VE"], 1)
                S.on("vector", down)
                ve_dn = S.inc("VE", 1)

                def act(sc, g=g, hraw=hraw, ve_dn=ve_dn):
                    W(sc, "VE", ve_dn)
                    sc.activation(out=COLS[:, 160:160 + SGRP],
                                  in_=hraw[:, g * SGRP:(g + 1) * SGRP],
                                  func=AF.Gelu).then_inc(SEM["SC"], 1)
                S.on("scalar", act)
                sc_g = S.inc("SC", 1)

                def ghm(ve, g=g, gh=gh, fsrbuf=fsrbuf, sc_g=sc_g):
                    W(ve, "SC", sc_g)
                    ve.tensor_mul(gh[:, g * SGRP:(g + 1) * SGRP],
                                  COLS[:, 160:160 + SGRP],
                                  fsrbuf[:, g * SGRP:(g + 1) * SGRP]).then_inc(SEM["VE"], 1)
                S.on("vector", ghm)
                ve_gh = S.inc("VE", 1)

                sb = SBL[g % 2]

                def sbuild(sc, g=g, gh=gh, sb=sb, ve_gh=ve_gh,
                           pe_gate=(pe_up[g - 2] if g >= 2 else 0)):
                    W(sc, "VE", ve_gh)
                    if pe_gate > 0:
                        W(sc, "PE", pe_gate)
                    li = None
                    for k in range(SGRP):
                        li = sc.mul(sb[:, k * 128:(k + 1) * 128], I128[:],
                                    gh[:, g * SGRP + k: g * SGRP + k + 1])
                    li.then_inc(SEM["SC"], 1)
                S.on("scalar", sbuild)
                sc_sb = S.inc("SC", 1)

                def up(te, g=g, sb=sb, gb=gb, sc_sb=sc_sb,
                       ve_psup=(ve_psup_free if g == 0 else 0)):
                    W(te, "SC", sc_sb)
                    if ve_psup > 0:
                        W(te, "VE", ve_psup)
                    li = None
                    for k in range(SGRP):
                        j = g * SGRP + k
                        first = (j == 0)
                        last = (j == NSLOT - 1)
                        li = te.matmul(ps_up[:, 0:512], sb[:, k * 128:(k + 1) * 128],
                                       gb[:, k * 2048 + 1024:k * 2048 + 1536],
                                       start=first, stop=last, skip_group_check=True)
                        li = te.matmul(ps_up[:, 512:1024], sb[:, k * 128:(k + 1) * 128],
                                       gb[:, k * 2048 + 1536:(k + 1) * 2048],
                                       start=first, stop=last, skip_group_check=True)
                    li.then_inc(SEM["PE"], 1)
                S.on("tensor", up)
                pe_up.append(S.inc("PE", 1))

            def ln2(ve, y1buf=y1buf, pe_done=pe_up[-1]):
                W(ve, "PE", pe_done)
                ve.tensor_add(U2[:], y1buf[:], ps_up[:])
                ve.reduce_sum(CS[:, 0:1], U2[:], axis=AX.X)
                ve.tensor_mul(U1[:], U2[:], U2[:])
                ve.reduce_sum(CS[:, 1:2], U1[:], axis=AX.X)
                ve.tensor_scalar(CS[:, 2:3], CS[:, 0:1], 1.0 / 1024, None, op0=ALU.mult)
                ve.tensor_scalar(CS[:, 3:4], CS[:, 1:2], 1.0 / 1024, None, op0=ALU.mult)
                ve.tensor_mul(CS[:, 4:5], CS[:, 2:3], CS[:, 2:3])
                ve.tensor_copy(CS[:, 15:16], CS[:, 0:1])
                ve.tensor_sub(CS[:, 5:6], CS[:, 3:4], CS[:, 4:5]).then_inc(SEM["VE"], 1)
            S.on("vector", ln2)
            ve_v2 = S.inc("VE", 1)

            def rst2(sc):
                W(sc, "VE", ve_v2)
                sc.activation(out=CS[:, 7:8], in_=CS[:, 5:6], func=AF.Sqrt,
                              bias=CS[:, 12:13], scale=1.0).then_inc(SEM["SC"], 1)
            S.on("scalar", rst2)
            sc_r2 = S.inc("SC", 1)

            def ln2b(ve, y2buf=y2buf, out_prev=out_prev):
                W(ve, "SC", sc_r2)
                if out_prev > 0:
                    W(ve, "OUT", out_prev)
                ve.reciprocal(CS[:, 6:7], CS[:, 7:8])
                ve.tensor_scalar(y2buf[:], U2[:], CS[:, 2:3], CS[:, 6:7],
                                 op0=ALU.subtract, op1=ALU.mult).then_inc(SEM["VE"], 1)
            S.on("vector", ln2b)
            ve_y2 = S.inc("VE", 1)

            def store(sy, ti=ti, y2buf=y2buf):
                W(sy, "VE", ve_y2)
                sy.dma_start(out=out[ti * 128:(ti + 1) * 128, :],
                             in_=y2buf[:]).then_inc(SEM["OUT"], 16)
            S.on("sync", store)
            out_cnt = S.inc("OUT", 16)

            if dbg:
                def dstore(sy, ti=ti, y1buf=y1buf, xnbuf=xnbuf, fibuf=fibuf,
                           fsrbuf=fsrbuf, hraw=hraw, ve_y2=ve_y2):
                    W(sy, "VE", ve_y2)
                    r = slice(ti * 128, (ti + 1) * 128)
                    sy.dma_start(out=dbg_t["y1d"][r, :], in_=y1buf[:]).then_inc(SEM["OUT"], 16)
                    sy.dma_start(out=dbg_t["fid"][r, :], in_=fibuf[:]).then_inc(SEM["OUT"], 16)
                    sy.dma_start(out=dbg_t["fsd"][r, :], in_=fsrbuf[:]).then_inc(SEM["OUT"], 16)
                    sy.dma_start(out=dbg_t["hrd"][r, :], in_=hraw[:]).then_inc(SEM["OUT"], 16)
                    sy.dma_start(out=dbg_t["xnd"][r, :], in_=xnbuf[:]).then_inc(SEM["OUT"], 16)
                    sy.dma_start(out=dbg_t["pbd"][r, :], in_=PB[ti % 2][:]).then_inc(SEM["OUT"], 16)
                S.on("sync", dstore)
                S.inc("OUT", 16 * 6)

            return (pe_up[-2], pe_up[-1]), ve_v2, out_cnt

        ve0 = ph_setup()
        prev_xfree = ve0
        psmm_gate = 0
        for pt in range(NPT):
            par = pt % 2
            ve_pb = ph_xload(x_pre, pt, par, prev_xfree, psmm_gate)
            ph_state_update(par, ve_pb, first=(pt == 0))
            prev_xfree = ve_pb

        gat_free = (0, 0)
        ve_psup_free = 0
        out_gate = [0, 0]
        prev_scxev = 0
        for ti in range(NT):
            par = ti % 2
            y1buf, xnbuf = Y1[ti % 3], XNB[ti % 3]
            xntbuf = XNT[par]
            fibuf, fsrbuf = FI[par], FSR[par]
            y2buf = Y2[par]
            ve_pb = ph_xload(x_own, ti, par, prev_xfree, psmm_gate, prev_scxev)
            ve_nud = ph_scan_tile(ti, par, ve_pb)
            ph_state_update(par, ve_nud, first=False)
            ve_xn, sc_xnt = ph_mixer_ln1(ti, par, ve_nud, y1buf, xnbuf, xntbuf)
            prev_xfree = ve_xn
            prev_scxev = sc_xnt
            if phase == "attn":
                def storea(sy, ti=ti, y1buf=y1buf, ve_xn=ve_xn, og=out_gate[par]):
                    W(sy, "VE", ve_xn)
                    if og > 0:
                        W(sy, "OUT", og)
                    sy.dma_start(out=out[ti * 128:(ti + 1) * 128, :],
                                 in_=y1buf[:]).then_inc(SEM["OUT"], 16)
                S.on("sync", storea)
                out_gate[par] = S.inc("OUT", 16)
                psmm_gate = ve_xn
                continue
            ve_route, ve_sims_last = ph_q_sims_routing(ti, par, sc_xnt, xntbuf,
                                                       fibuf, fsrbuf)
            psmm_gate = ve_sims_last
            if dbg and ti == 3:
                def fdump(sy, fibuf=fibuf, fsrbuf=fsrbuf, ve_route=ve_route):
                    W(sy, "VE", ve_route)
                    sy.dma_start(out=dbg_t["fid"][0:128, :], in_=fibuf[:]).then_inc(SEM["OUT"], 16)
                    sy.dma_start(out=dbg_t["fsd"][0:128, :], in_=fsrbuf[:]).then_inc(SEM["OUT"], 16)
                S.on("sync", fdump)
                S.inc("OUT", 32)
            if phase == "route":
                def storea(sy, ti=ti, y1buf=y1buf, fibuf=fibuf, fsrbuf=fsrbuf,
                           ve_route=ve_route, og=out_gate[par]):
                    W(sy, "VE", ve_route)
                    if og > 0:
                        W(sy, "OUT", og)
                    sy.dma_start(out=out[ti * 128:(ti + 1) * 128, :],
                                 in_=y1buf[:]).then_inc(SEM["OUT"], 16)
                S.on("sync", storea)
                out_gate[par] = S.inc("OUT", 16)
                continue
            gat_free, ve_psup_free, oc = ph_ffn_tile(
                ti, par, ve_route, xnbuf, y1buf, fibuf, fsrbuf, y2buf,
                gat_free, ve_psup_free, out_gate[par])
            out_gate[par] = oc

        out_total = S.get("OUT")

        def fin(sy):
            sy.wait_ge(SEM["OUT"], out_total)
        S.on("sync", fin)

        block = E(nc.Block())

        @block.sync
        def _(sync):
            for fn in S.ops["sync"]:
                fn(sync)

        @block.vector
        def _(vector):
            for fn in S.ops["vector"]:
                fn(vector)

        @block.scalar
        def _(scalar):
            for fn in S.ops["scalar"]:
                fn(scalar)

        @block.tensor
        def _(tensor):
            for fn in S.ops["tensor"]:
                fn(tensor)

        @block.gpsimd
        def _(gpsimd):
            for fn in S.ops["gpsimd"]:
                fn(gpsimd)

    return nc


_NC_CACHE = {}


def _get_nc(dbg=False):
    if dbg not in _NC_CACHE:
        _NC_CACHE[dbg] = build_program(dbg)
    return _NC_CACHE[dbg]


def make_in_maps(L0_attn_in_sequence_bwc, L0_attn_W_proj, L0_attn_pre_metric_nkk,
                 L0_attn_transforms_nkk, L0_attn_W_mixer, L0_ffn_w_q, L0_ffn_keys,
                 L0_ffn_expert_down, L0_ffn_expert_up, L0_ffn_norm_w):
    x = np.ascontiguousarray(np.asarray(L0_attn_in_sequence_bwc, dtype=np.float32))
    wpT = np.ascontiguousarray(np.asarray(L0_attn_W_proj, np.float32).T)
    wmT = np.ascontiguousarray(np.asarray(L0_attn_W_mixer, np.float32).T)
    wqT = np.ascontiguousarray(np.asarray(L0_ffn_w_q, np.float32).T)
    keys = np.asarray(L0_ffn_keys, np.float32)
    keysT = np.ascontiguousarray(
        keys.transpose(0, 2, 3, 1).reshape(16, 128, 256))
    pm = np.asarray(L0_attn_pre_metric_nkk, np.float32)
    pmT = np.ascontiguousarray(pm.transpose(0, 2, 1))
    trfa = np.ascontiguousarray(np.asarray(L0_attn_transforms_nkk, np.float32))
    nw = np.ascontiguousarray(np.asarray(L0_ffn_norm_w, np.float32).reshape(1, 1024))
    comb = np.ascontiguousarray(np.concatenate(
        [np.asarray(L0_ffn_expert_down, np.float32),
         np.asarray(L0_ffn_expert_up, np.float32)], axis=1))
    zeros_pre = np.zeros((2048, 1024), np.float32)

    in_maps = []
    for c in range(8):
        b, h2 = c // 2, c % 2
        x_ownc = np.ascontiguousarray(x[b, h2 * 2048:(h2 + 1) * 2048, :])
        x_prec = np.ascontiguousarray(x[b, 0:2048, :]) if h2 == 1 else zeros_pre
        in_maps.append(dict(x_own=x_ownc, x_pre=x_prec, wpT=wpT, wmT=wmT,
                            wqT=wqT, keysT=keysT, pmT=pmT, trf=trfa, nw_in=nw,
                            comb=comb))
    return in_maps


def kernel(**inputs):
    in_maps = make_in_maps(**inputs)
    nc = _get_nc(False)
    res = run_bass_kernel_spmd(nc, in_maps, core_ids=list(range(8)))
    outp = np.empty((4, 4096, 1024), np.float32)
    for c in range(8):
        b, h2 = c // 2, c % 2
        outp[b, h2 * 2048:(h2 + 1) * 2048, :] = res.results[c]["out"]
    return outp


# version 02
# speedup vs baseline: 1.0291x; 1.0291x over previous
import numpy as np
from contextlib import ExitStack

import concourse.bass as bass
import concourse.mybir as mybir
from concourse.bass_utils import run_bass_kernel_spmd
from concourse.bass_core_rust import IndirectOffsetOnAxis

F32 = mybir.dt.float32
BF16 = mybir.dt.bfloat16
I32 = mybir.dt.int32
U16 = mybir.dt.uint16
U32 = mybir.dt.uint32
AF = mybir.ActivationFunctionType
ALU = mybir.AluOpType
AX = mybir.AxisListType

NT = 16
NPT = 16
NSLOT = 64
SGRP = 4
NGRP = NSLOT // SGRP


class Sched:
    ENGINES = ("sync", "vector", "scalar", "tensor", "gpsimd")

    def __init__(self):
        self.ops = {e: [] for e in self.ENGINES}
        self.cnt = {}

    def inc(self, sem, by):
        self.cnt[sem] = self.cnt.get(sem, 0) + by
        return self.cnt[sem]

    def get(self, sem):
        return self.cnt.get(sem, 0)

    def on(self, eng, fn):
        self.ops[eng].append(fn)


def build_program(dbg=False, phase="full"):
    nc = bass.Bass("TRN2", target_bir_lowering=False)

    x_own = nc.dram_tensor("x_own", [2048, 1024], F32, kind="ExternalInput")
    x_pre = nc.dram_tensor("x_pre", [2048, 1024], F32, kind="ExternalInput")
    wpT = nc.dram_tensor("wpT", [1024, 1024], F32, kind="ExternalInput")
    wmT = nc.dram_tensor("wmT", [1024, 1024], F32, kind="ExternalInput")
    wqT = nc.dram_tensor("wqT", [1024, 2048], F32, kind="ExternalInput")
    keysT = nc.dram_tensor("keysT", [16, 128, 256], F32, kind="ExternalInput")
    pmT = nc.dram_tensor("pmT", [16, 64, 64], F32, kind="ExternalInput")
    trf = nc.dram_tensor("trf", [16, 64, 64], F32, kind="ExternalInput")
    nw_in = nc.dram_tensor("nw_in", [1, 1024], F32, kind="ExternalInput")
    comb = nc.dram_tensor("comb", [65536, 2048], F32, kind="ExternalInput")
    out = nc.dram_tensor("out", [2048, 1024], F32, kind="ExternalOutput")
    dbg_t = {}
    if dbg:
        for nm, sh in (("y1d", [2048, 1024]), ("fid", [2048, 64]),
                       ("fsd", [2048, 64]), ("hrd", [2048, 64]),
                       ("xnd", [2048, 1024]), ("pbd", [2048, 1024])):
            dt_ = {"fid": U32, "xnd": BF16, "pbd": BF16}.get(nm, F32)
            dbg_t[nm] = nc.dram_tensor(nm, sh, dt_, kind="ExternalOutput")
        dbg_t["simd"] = nc.dram_tensor("simd", [2048, 512], F32, kind="ExternalOutput")
        dbg_t["csd"] = nc.dram_tensor("csd", [2048, 16], F32, kind="ExternalOutput")
        dbg_t["cod"] = nc.dram_tensor("cod", [128, 192], F32, kind="ExternalOutput")
        dbg_t["cud"] = nc.dram_tensor("cud", [128, 24], U16, kind="ExternalOutput")
        dbg_t["cid"] = nc.dram_tensor("cid", [128, 24], I32, kind="ExternalOutput")
        dbg_t["nwd"] = nc.dram_tensor("nwd", [128, 1024], F32, kind="ExternalOutput")
        dbg_t["qtd"] = nc.dram_tensor("qtd", [2048, 256], BF16, kind="ExternalOutput")

    with ExitStack() as ctx:
        E = ctx.enter_context
        WPT = E(nc.sbuf_tensor("WPT", [128, 8192], BF16))
        WQT = E(nc.sbuf_tensor("WQT", [128, 16384], BF16))
        WMT = E(nc.sbuf_tensor("WMT", [128, 8192], BF16))
        KEYS = E(nc.sbuf_tensor("KEYS", [128, 4096], BF16))
        PMTS = E(nc.sbuf_tensor("PMTS", [128, 1024], BF16))
        MSB = E(nc.sbuf_tensor("MSB", [128, 1024], BF16))
        TRFS = E(nc.sbuf_tensor("TRFS", [128, 1024], BF16))
        NWR = E(nc.sbuf_tensor("NWR", [128, 1024], F32))
        IOTAF = E(nc.sbuf_tensor("IOTAF", [128, 128], F32))
        ROWID = E(nc.sbuf_tensor("ROWID", [128, 1], F32))
        IOTA8 = E(nc.sbuf_tensor("IOTA8", [128, 8], F32))
        I128 = E(nc.sbuf_tensor("I128", [128, 128], BF16))
        TRIL = E(nc.sbuf_tensor("TRIL", [128, 128], BF16))

        XF32 = [E(nc.sbuf_tensor(f"XF32_{i}", [128, 1024], F32)) for i in range(2)]
        XB = [E(nc.sbuf_tensor(f"XB_{i}", [128, 1024], BF16)) for i in range(2)]
        XTC = [E(nc.sbuf_tensor(f"XTC_{i}", [128, 1024], BF16)) for i in range(2)]
        PB = [E(nc.sbuf_tensor(f"PB_{i}", [128, 1024], BF16)) for i in range(2)]
        PTC = [E(nc.sbuf_tensor(f"PTC_{i}", [128, 2048], BF16)) for i in range(2)]
        QMSB = E(nc.sbuf_tensor("QMSB", [128, 256], BF16))
        SSB = E(nc.sbuf_tensor("SSB", [128, 1024], BF16))
        AMSK = E(nc.sbuf_tensor("AMSK", [128, 256], BF16))
        OTSB = E(nc.sbuf_tensor("OTSB", [128, 256], BF16))
        NUD = E(nc.sbuf_tensor("NUD", [128, 1024], BF16))
        U1 = E(nc.sbuf_tensor("U1", [128, 1024], F32))
        U2 = E(nc.sbuf_tensor("U2", [128, 1024], F32))
        Y1 = [E(nc.sbuf_tensor(f"Y1_{i}", [128, 1024], F32)) for i in range(3)]
        XNB = [E(nc.sbuf_tensor(f"XNB_{i}", [128, 1024], BF16)) for i in range(3)]
        XNT = [E(nc.sbuf_tensor(f"XNT_{i}", [128, 1024], BF16)) for i in range(2)]
        QTT = [E(nc.sbuf_tensor("QTT_0", [128, 2048], BF16))]
        SIMS = E(nc.sbuf_tensor("SIMS", [128, 512], F32))
        COLS = E(nc.sbuf_tensor("COLS", [128, 192], F32))
        COLI = E(nc.sbuf_tensor("COLI", [128, 24], I32))
        COLU = E(nc.sbuf_tensor("COLU", [128, 24], U16))
        OH = E(nc.sbuf_tensor("OH", [128, 64], F32))
        FI = [E(nc.sbuf_tensor(f"FI_{i}", [128, 64], U32)) for i in range(2)]
        FSR = [E(nc.sbuf_tensor(f"FSR_{i}", [128, 64], F32)) for i in range(2)]
        HRAW = [E(nc.sbuf_tensor(f"HRAW_{i}", [128, 64], F32)) for i in range(2)]
        GH = [E(nc.sbuf_tensor(f"GH_{i}", [128, 64], F32)) for i in range(2)]
        GAT = [E(nc.sbuf_tensor(f"GAT_{i}", [128, SGRP * 2048], BF16))
               for i in range(2)]
        SBL = [E(nc.sbuf_tensor(f"SBL_{i}", [128, SGRP * 128], BF16))
               for i in range(2)]
        I1A = E(nc.sbuf_tensor("I1A", [128, 64], F32))
        I2A = E(nc.sbuf_tensor("I2A", [128, 64], F32))
        PKA = E(nc.sbuf_tensor("PKA", [128, 64], F32))
        FSA = E(nc.sbuf_tensor("FSA", [128, 64], F32))
        OHW = E(nc.sbuf_tensor("OHW", [128, 512], F32))
        I1S = E(nc.sbuf_tensor("I1S", [128, 64], F32))
        I2S = E(nc.sbuf_tensor("I2S", [128, 64], F32))
        Y2 = [E(nc.sbuf_tensor(f"Y2_{i}", [128, 1024], F32)) for i in range(2)]
        CS = E(nc.sbuf_tensor("CS", [128, 16], F32))

        ps_mm = E(nc.psum_tensor("ps_mm", [128, 1024], F32))
        ps_up = E(nc.psum_tensor("ps_up", [128, 1024], F32))
        ps_state = E(nc.psum_tensor("ps_state", [64, 1024], F32))
        ps_scan = E(nc.psum_tensor("ps_scan", [128, 512], F32))
        ps_tr = E(nc.psum_tensor("ps_tr", [128, 1024], BF16))

        sem_names = ["LX", "LW", "G", "OUT", "PE", "VE", "SC", "GP"]
        SEM = {n: E(nc.semaphore(n)) for n in sem_names}
        S = Sched()

        def W(eng, sem, val):
            if val > 0:
                eng.wait_ge(SEM[sem], val)

        def ph_setup():
            def g_load(gp):
                gp.dma_start(out=WPT[:].rearrange("p (dc j) -> p dc j", j=1024),
                             in_=wpT[:].rearrange("(dc p) j -> p dc j", p=128)).then_inc(SEM["LW"], 16)
                gp.dma_start(out=WMT[:].rearrange("p (dc j) -> p dc j", j=1024),
                             in_=wmT[:].rearrange("(dc p) j -> p dc j", p=128)).then_inc(SEM["LW"], 16)
                gp.dma_start(out=WQT[:].rearrange("p (dc j) -> p dc j", j=2048),
                             in_=wqT[:].rearrange("(dc p) j -> p dc j", p=128)).then_inc(SEM["LW"], 16)
                gp.dma_start(out=KEYS[:].rearrange("p (c n) -> p c n", n=256),
                             in_=keysT[:].rearrange("c p n -> p c n")).then_inc(SEM["LW"], 16)
                gp.dma_start(out=PMTS[0:64, :].rearrange("p (h i) -> p h i", i=64),
                             in_=pmT[:].rearrange("h p i -> p h i")).then_inc(SEM["LW"], 16)
                gp.dma_start(out=TRFS[0:64, :].rearrange("p (h j) -> p h j", j=64),
                             in_=trf[:].rearrange("h p j -> p h j")).then_inc(SEM["LW"], 16)
                gp.dma_start(out=NWR[:],
                             in_=bass.AP(nw_in, 0, [[0, 128], [1, 1024]])).then_inc(SEM["LW"], 16)
                gp.iota(IOTAF[:], [[1, 128]], channel_multiplier=0,
                        allow_small_or_imprecise_dtypes=True)
                gp.iota(ROWID[:], [[0, 1]], channel_multiplier=1,
                        allow_small_or_imprecise_dtypes=True)
                gp.iota(IOTA8[:], [[1, 8]], channel_multiplier=0,
                        allow_small_or_imprecise_dtypes=True).then_inc(SEM["GP"], 1)
            S.on("gpsimd", g_load)
            lw_cnt = S.inc("LW", 16 * 7)
            gp_cnt = S.inc("GP", 1)

            def v_consts(ve):
                W(ve, "GP", gp_cnt)
                ve.tensor_scalar(I128[:], IOTAF[:], ROWID[:, 0:1], None, op0=ALU.is_equal)
                ve.tensor_scalar(TRIL[:], IOTAF[:], ROWID[:, 0:1], None, op0=ALU.is_ge)
                ve.memset(CS[:, 12:13], 1e-5)
                ve.memset(CS[:, 13:14], 1e-6).then_inc(SEM["VE"], 1)
            S.on("vector", v_consts)
            ve_c = S.inc("VE", 1)

            def t_setup(te):
                W(te, "LW", lw_cnt)
                W(te, "VE", ve_c)
                for h in range(16):
                    te.matmul(ps_state[0:64, h * 64:(h + 1) * 64],
                              PMTS[0:64, h * 64:(h + 1) * 64],
                              PMTS[0:64, h * 64:(h + 1) * 64],
                              start=True, stop=True)
                te.matmul(ps_scan[0:64, 0:64], PMTS[0:64, 0:64], PMTS[0:64, 0:64],
                          start=True, stop=True).then_inc(SEM["PE"], 1)
            S.on("tensor", t_setup)
            pe_c = S.inc("PE", 1)

            def v_setup2(ve):
                W(ve, "PE", pe_c)
                ve.tensor_copy(MSB[0:64, :], ps_state[0:64, :]).then_inc(SEM["VE"], 1)
            S.on("vector", v_setup2)
            return S.inc("VE", 1)

        def ph_xload(src, ti, par, wait_ve_xbuf, wait_ve_psmm, wait_sc_pstr=0):
            def ld(sy):
                W(sy, "VE", wait_ve_xbuf)
                sy.dma_start(out=XF32[par][:],
                             in_=src[ti * 128:(ti + 1) * 128, :]).then_inc(SEM["LX"], 16)
            S.on("sync", ld)
            lx = S.inc("LX", 16)

            def cast(ve):
                W(ve, "LX", lx)
                ve.tensor_copy(XB[par][:], XF32[par][:]).then_inc(SEM["VE"], 1)
            S.on("vector", cast)
            ve_cast = S.inc("VE", 1)

            def tr(te):
                W(te, "VE", ve_cast)
                if wait_sc_pstr > 0:
                    W(te, "SC", wait_sc_pstr)
                li = None
                for dc in range(8):
                    li = te.transpose(ps_tr[:, dc * 128:(dc + 1) * 128],
                                      XB[par][:, dc * 128:(dc + 1) * 128], I128[:])
                li.then_inc(SEM["PE"], 1)
            S.on("tensor", tr)
            pe_tr = S.inc("PE", 1)

            def ev(sc):
                W(sc, "PE", pe_tr)
                sc.copy(XTC[par][:], ps_tr[:]).then_inc(SEM["SC"], 1)
            S.on("scalar", ev)
            sc_ev = S.inc("SC", 1)

            def proj(te):
                W(te, "SC", sc_ev)
                W(te, "VE", wait_ve_psmm)
                li = None
                for half in range(2):
                    for dc in range(8):
                        li = te.matmul(
                            ps_mm[:, half * 512:(half + 1) * 512],
                            XTC[par][:, dc * 128:(dc + 1) * 128],
                            WPT[:, dc * 1024 + half * 512: dc * 1024 + (half + 1) * 512],
                            start=(dc == 0), stop=(dc == 7))
                li.then_inc(SEM["PE"], 1)
            S.on("tensor", proj)
            pe_proj = S.inc("PE", 1)

            def pb(ve):
                W(ve, "PE", pe_proj)
                ve.tensor_copy(PB[par][:], ps_mm[:]).then_inc(SEM["VE"], 1)
            S.on("vector", pb)
            ve_pb = S.inc("VE", 1)
            return ve_pb

        def ph_state_update(par, wait_ve, first):
            def st(te):
                W(te, "VE", wait_ve)
                li = None
                for h in range(16):
                    li = te.matmul(ps_state[0:64, h * 64:(h + 1) * 64],
                                   PB[par][:, h * 64:(h + 1) * 64],
                                   PB[par][:, h * 64:(h + 1) * 64],
                                   start=(first and h in (0, 8)), stop=False,
                                   skip_group_check=True)
                li.then_inc(SEM["PE"], 1)
            S.on("tensor", st)
            return S.inc("PE", 1)

        def ph_scan_tile(ti, par, ve_pb):
            def ptr_a(te):
                W(te, "VE", ve_pb)
                li = None
                for h in range(8):
                    li = te.transpose(ps_tr[0:64, h * 128:(h + 1) * 128],
                                      PB[par][:, h * 64:(h + 1) * 64], I128[:])
                li.then_inc(SEM["PE"], 1)
            S.on("tensor", ptr_a)
            pe_ptr_a = S.inc("PE", 1)

            def ptev_a(sc):
                W(sc, "PE", pe_ptr_a)
                sc.copy(PTC[par][0:64, 0:1024], ps_tr[0:64, 0:1024]).then_inc(SEM["SC"], 1)
            S.on("scalar", ptev_a)
            sc_pt_a = S.inc("SC", 1)

            def ptr_b(te):
                W(te, "SC", sc_pt_a)
                li = None
                for h in range(8, 16):
                    li = te.transpose(ps_tr[0:64, (h - 8) * 128:(h - 7) * 128],
                                      PB[par][:, h * 64:(h + 1) * 64], I128[:])
                li.then_inc(SEM["PE"], 1)
            S.on("tensor", ptr_b)
            pe_ptr_b = S.inc("PE", 1)

            def ptev_b(sc):
                W(sc, "PE", pe_ptr_b)
                sc.copy(PTC[par][0:64, 1024:2048], ps_tr[0:64, 0:1024]).then_inc(SEM["SC"], 1)
            S.on("scalar", ptev_b)
            sc_pt = S.inc("SC", 1)

            def ssnap(ve):
                ve.tensor_copy(SSB[0:64, :], ps_state[0:64, :]).then_inc(SEM["VE"], 1)
            S.on("vector", ssnap)
            ve_snap = S.inc("VE", 1)

            last_ve = ve_snap
            ve_e3 = 0
            for h in range(16):
                hp = h % 2

                def s1(te, h=h, first=(h == 0), sc_pt=sc_pt, ve_snap=ve_snap):
                    if first:
                        W(te, "SC", sc_pt)
                        W(te, "VE", ve_snap)
                    te.matmul(ps_scan[0:64, 128:256],
                              MSB[0:64, h * 64:(h + 1) * 64],
                              PTC[par][0:64, h * 128:(h + 1) * 128],
                              start=True, stop=True).then_inc(SEM["PE"], 1)
                S.on("tensor", s1)
                pe_s1 = S.inc("PE", 1)

                def e1(ve, h=h, pe_s1=pe_s1):
                    W(ve, "PE", pe_s1)
                    ve.tensor_copy(QMSB[0:64, (h % 2) * 128:(h % 2 + 1) * 128],
                                   ps_scan[0:64, 128:256]).then_inc(SEM["VE"], 1)
                S.on("vector", e1)
                ve_e1 = S.inc("VE", 1)

                def s2(te, h=h, ve_e1=ve_e1):
                    W(te, "VE", ve_e1)
                    te.matmul(ps_scan[:, 0:128],
                              PTC[par][0:64, h * 128:(h + 1) * 128],
                              QMSB[0:64, (h % 2) * 128:(h % 2 + 1) * 128],
                              start=True, stop=True).then_inc(SEM["PE"], 1)
                S.on("tensor", s2)
                pe_s2 = S.inc("PE", 1)

                def msk(ve, hp=hp, pe_s2=pe_s2):
                    W(ve, "PE", pe_s2)
                    ve.tensor_mul(AMSK[:, hp * 128:(hp + 1) * 128],
                                  ps_scan[:, 0:128], TRIL[:]).then_inc(SEM["VE"], 1)
                S.on("vector", msk)
                ve_msk = S.inc("VE", 1)

                def s34(te, h=h, hp=hp, ve_msk=ve_msk):
                    W(te, "VE", ve_msk)
                    te.matmul(ps_scan[0:64, 256:384],
                              SSB[0:64, h * 64:(h + 1) * 64],
                              QMSB[0:64, (h % 2) * 128:(h % 2 + 1) * 128],
                              start=True, stop=False, skip_group_check=True)
                    te.matmul(ps_scan[0:64, 256:384],
                              PB[par][:, h * 64:(h + 1) * 64],
                              AMSK[:, hp * 128:(hp + 1) * 128],
                              start=False, stop=True,
                              skip_group_check=True).then_inc(SEM["PE"], 1)
                S.on("tensor", s34)
                pe_s34 = S.inc("PE", 1)

                def e2(sc, h=h, pe_s34=pe_s34):
                    W(sc, "PE", pe_s34)
                    sc.mul(OTSB[0:64, (h % 2) * 128:(h % 2 + 1) * 128],
                           ps_scan[0:64, 256:384], 0.125).then_inc(SEM["SC"], 1)
                S.on("scalar", e2)
                sc_e2 = S.inc("SC", 1)

                def s5(te, h=h, sc_e2=sc_e2, ve_e3=ve_e3):
                    W(te, "SC", sc_e2)
                    if h % 2 == 0 and ve_e3 > 0:
                        W(te, "VE", ve_e3)
                    tp = (0, 64) if h % 2 == 1 else (0, 0)
                    dst = (ps_scan[64:128, 384:512] if h % 2 == 1
                           else ps_scan[0:64, 384:512])
                    te.matmul(dst, TRFS[0:64, h * 64:(h + 1) * 64],
                              OTSB[0:64, (h % 2) * 128:(h % 2 + 1) * 128],
                              start=True, stop=True,
                              tile_position=tp).then_inc(SEM["PE"], 1)
                S.on("tensor", s5)
                pe_s5 = S.inc("PE", 1)

                if h % 2 == 1:
                    def e3(ve, h=h, pe_s5=pe_s5):
                        W(ve, "PE", pe_s5)
                        ve.tensor_copy(NUD[:, (h // 2) * 128:(h // 2 + 1) * 128],
                                       ps_scan[:, 384:512]).then_inc(SEM["VE"], 1)
                    S.on("vector", e3)
                    ve_e3 = S.inc("VE", 1)
                    last_ve = ve_e3
            return last_ve

        def ph_mixer_ln1(ti, par, ve_nud, y1buf, xnbuf, xntbuf):
            def mx(te):
                W(te, "VE", ve_nud)
                li = None
                for half in range(2):
                    for c in range(8):
                        li = te.matmul(
                            ps_mm[:, half * 512:(half + 1) * 512],
                            NUD[:, c * 128:(c + 1) * 128],
                            WMT[:, c * 1024 + half * 512: c * 1024 + (half + 1) * 512],
                            start=(c == 0), stop=(c == 7))
                li.then_inc(SEM["PE"], 1)
            S.on("tensor", mx)
            pe_mx = S.inc("PE", 1)

            def ln1(ve):
                W(ve, "PE", pe_mx)
                ve.tensor_add(U1[:], XF32[par][:], ps_mm[:])
                ve.reduce_sum(CS[:, 0:1], U1[:], axis=AX.X)
                ve.tensor_mul(U2[:], U1[:], U1[:])
                ve.reduce_sum(CS[:, 1:2], U2[:], axis=AX.X)
                ve.tensor_scalar(CS[:, 2:3], CS[:, 0:1], 1.0 / 1024, None, op0=ALU.mult)
                ve.tensor_scalar(CS[:, 3:4], CS[:, 1:2], 1.0 / 1024, None, op0=ALU.mult)
                ve.tensor_mul(CS[:, 4:5], CS[:, 2:3], CS[:, 2:3])
                ve.tensor_copy(CS[:, 15:16], CS[:, 0:1])
                ve.tensor_sub(CS[:, 5:6], CS[:, 3:4], CS[:, 4:5]).then_inc(SEM["VE"], 1)
            S.on("vector", ln1)
            ve_st = S.inc("VE", 1)

            def rst(sc):
                W(sc, "VE", ve_st)
                sc.activation(out=CS[:, 7:8], in_=CS[:, 5:6], func=AF.Sqrt,
                              bias=CS[:, 12:13], scale=1.0).then_inc(SEM["SC"], 1)
            S.on("scalar", rst)
            sc_rst = S.inc("SC", 1)

            def ln1b(ve):
                W(ve, "SC", sc_rst)
                ve.reciprocal(CS[:, 6:7], CS[:, 7:8])
                ve.tensor_scalar(y1buf[:], U1[:], CS[:, 2:3], CS[:, 6:7],
                                 op0=ALU.subtract, op1=ALU.mult)
                ve.tensor_mul(U2[:], y1buf[:], y1buf[:])
                ve.reduce_sum(CS[:, 8:9], U2[:], axis=AX.X)
                ve.tensor_copy(CS[:, 15:16], CS[:, 0:1])
                ve.tensor_scalar(CS[:, 9:10], CS[:, 8:9], 1.0 / 1024, None,
                                 op0=ALU.mult).then_inc(SEM["VE"], 1)
            S.on("vector", ln1b)
            ve_ms = S.inc("VE", 1)

            def rin(sc):
                W(sc, "VE", ve_ms)
                sc.activation(out=CS[:, 11:12], in_=CS[:, 9:10], func=AF.Sqrt,
                              bias=CS[:, 13:14], scale=1.0).then_inc(SEM["SC"], 1)
            S.on("scalar", rin)
            sc_rin = S.inc("SC", 1)

            def xn(ve):
                W(ve, "SC", sc_rin)
                ve.reciprocal(CS[:, 10:11], CS[:, 11:12])
                ve.tensor_copy(CS[:, 15:16], CS[:, 0:1])
                ve.tensor_scalar(U2[:], y1buf[:], CS[:, 10:11], None, op0=ALU.mult)
                ve.tensor_mul(xnbuf[:], U2[:], NWR[:]).then_inc(SEM["VE"], 1)
            S.on("vector", xn)
            ve_xn = S.inc("VE", 1)

            if dbg and ti == 3:
                def xdump(sy, xnbuf=xnbuf, y1buf=y1buf, ve_xn=ve_xn):
                    W(sy, "VE", ve_xn)
                    sy.dma_start(out=dbg_t["xnd"][0:128, :], in_=xnbuf[:]).then_inc(SEM["OUT"], 16)
                    sy.dma_start(out=dbg_t["y1d"][0:128, :], in_=y1buf[:]).then_inc(SEM["OUT"], 16)
                    sy.dma_start(out=dbg_t["csd"][0:128, :], in_=CS[:]).then_inc(SEM["OUT"], 16)
                    sy.dma_start(out=dbg_t["nwd"][:], in_=NWR[:]).then_inc(SEM["OUT"], 16)
                S.on("sync", xdump)
                dump_out = S.inc("OUT", 64)
            else:
                dump_out = 0

            def xtr(te):
                if dump_out > 0:
                    W(te, "OUT", dump_out)
                W(te, "VE", ve_xn)
                li = None
                for dc in range(8):
                    li = te.transpose(ps_tr[:, dc * 128:(dc + 1) * 128],
                                      xnbuf[:, dc * 128:(dc + 1) * 128], I128[:])
                li.then_inc(SEM["PE"], 1)
            S.on("tensor", xtr)
            pe_xtr = S.inc("PE", 1)

            def xev(sc):
                W(sc, "PE", pe_xtr)
                sc.copy(xntbuf[:], ps_tr[:]).then_inc(SEM["SC"], 1)
            S.on("scalar", xev)
            sc_xev = S.inc("SC", 1)
            return ve_xn, sc_xev

        def ph_q_sims_routing(ti, par, sc_xnt, xntbuf, fibuf, fsrbuf):
            qt = QTT[0]
            ve_qprev = 0
            for grp in range(2):
                def qmm(te, grp=grp, sc_xnt=sc_xnt, ve_qprev=ve_qprev):
                    if grp == 0:
                        W(te, "SC", sc_xnt)
                    else:
                        W(te, "VE", ve_qprev)
                    li = None
                    for qs in range(8):
                        qq = grp * 8 + qs
                        for dc in range(8):
                            li = te.matmul(
                                ps_mm[:, qs * 128:(qs + 1) * 128],
                                WQT[:, dc * 2048 + qq * 128: dc * 2048 + (qq + 1) * 128],
                                xntbuf[:, dc * 128:(dc + 1) * 128],
                                start=(dc == 0), stop=(dc == 7))
                    li.then_inc(SEM["PE"], 1)
                S.on("tensor", qmm)
                pe_q = S.inc("PE", 1)

                def qev(ve, grp=grp, pe_q=pe_q):
                    W(ve, "PE", pe_q)
                    ve.tensor_copy(qt[:, grp * 1024:(grp + 1) * 1024],
                                   ps_mm[:]).then_inc(SEM["VE"], 1)
                S.on("vector", qev)
                ve_qprev = S.inc("VE", 1)
            ve_qdone = ve_qprev

            ve_last = ve_qdone
            ve_sev_prev = 0
            for h in range(8):
                def smm(te, h=h, ve_qdone=ve_qdone, ve_sev_prev=ve_sev_prev):
                    if h == 0:
                        W(te, "VE", ve_qdone)
                    else:
                        W(te, "VE", ve_sev_prev)
                    li = None
                    for which in range(2):
                        qs = h * 2 + which
                        li = te.matmul(ps_mm[:, which * 256:(which + 1) * 256],
                                       qt[:, qs * 128:(qs + 1) * 128],
                                       KEYS[:, qs * 256:(qs + 1) * 256],
                                       start=True, stop=True)
                    li.then_inc(SEM["PE"], 1)
                S.on("tensor", smm)
                pe_sm = S.inc("PE", 1)

                def sev(ve, pe_sm=pe_sm):
                    W(ve, "PE", pe_sm)
                    ve.tensor_copy(SIMS[:], ps_mm[:, 0:512]).then_inc(SEM["VE"], 1)
                S.on("vector", sev)
                ve_sev_prev = S.inc("VE", 1)
                sdump_out = 0
                if dbg and h == 0 and ti == 3:
                    def sdump(sy, ve_sev=ve_sev_prev):
                        W(sy, "VE", ve_sev)
                        sy.dma_start(out=dbg_t["simd"][0:128, :],
                                     in_=SIMS[:]).then_inc(SEM["OUT"], 16)
                        sy.dma_start(out=dbg_t["qtd"][0:128, :],
                                     in_=QTT[0][:, 0:256]).then_inc(SEM["OUT"], 16)
                    S.on("sync", sdump)
                    sdump_out = S.inc("OUT", 32)

                def route(ve, h=h, sdump_out=sdump_out):
                    if sdump_out > 0:
                        W(ve, "OUT", sdump_out)
                    s1t, s2t = COLS[:, 0:8], COLS[:, 8:16]
                    i1u, i2u, pku = COLU[:, 0:8], COLU[:, 8:16], COLU[:, 16:24]
                    ps64 = COLS[:, 96:160]
                    ve.max(out=s1t, in_=SIMS[:, 0:256])
                    ve.tensor_copy(COLS[:, 80:88], s1t)
                    ve.max_index(out=i1u, in_max=s1t, in_values=SIMS[:, 0:256])
                    ve.max(out=s2t, in_=SIMS[:, 256:512])
                    ve.tensor_copy(COLS[:, 80:88], s2t)
                    ve.max_index(out=i2u, in_max=s2t, in_values=SIMS[:, 256:512])
                    ve.tensor_tensor(ps64.rearrange("p (a b) -> p a b", b=8),
                                     s1t.unsqueeze(2).to_broadcast([128, 8, 8]),
                                     s2t.unsqueeze(1).to_broadcast([128, 8, 8]),
                                     op=ALU.add)
                    ve.max(out=FSA[:, h * 8:(h + 1) * 8], in_=ps64)
                    ve.tensor_copy(COLS[:, 80:88], FSA[:, h * 8:(h + 1) * 8])
                    ve.max_index(out=pku, in_max=FSA[:, h * 8:(h + 1) * 8],
                                 in_values=ps64)
                    ve.tensor_copy(I1A[:, h * 8:(h + 1) * 8], i1u)
                    ve.tensor_copy(I2A[:, h * 8:(h + 1) * 8], i2u)
                    ve.tensor_copy(PKA[:, h * 8:(h + 1) * 8],
                                   pku).then_inc(SEM["VE"], 1)
                S.on("vector", route)
                ve_last = S.inc("VE", 1)
                if dbg and h == 0 and ti == 3:
                    def rdump(sy, ve_r=ve_last):
                        W(sy, "VE", ve_r)
                        sy.dma_start(out=dbg_t["cod"][:], in_=COLS[:]).then_inc(SEM["OUT"], 16)
                        sy.dma_start(out=dbg_t["cud"][:], in_=COLU[:]).then_inc(SEM["OUT"], 16)
                        sy.dma_start(out=dbg_t["cid"][:], in_=COLI[:]).then_inc(SEM["OUT"], 16)
                    S.on("sync", rdump)
                    rdump_out = S.inc("OUT", 48)
                else:
                    rdump_out = 0
                if rdump_out:
                    def rgate(ve, rdump_out=rdump_out):
                        W(ve, "OUT", rdump_out)
                    S.on("vector", rgate)

            def wtail(ve, fibuf=fibuf, fsrbuf=fsrbuf):
                C = 12582912.0
                oh4 = OHW[:].rearrange("p (h j c) -> p h j c", j=8, c=8)
                io4 = IOTA8[:].unsqueeze(1).unsqueeze(2).to_broadcast([128, 8, 8, 8])
                af4 = I1S[:].rearrange("p (h j) -> p h j", j=8).unsqueeze(3)\
                          .to_broadcast([128, 8, 8, 8])
                bf4 = I2S[:].rearrange("p (h j) -> p h j", j=8).unsqueeze(3)\
                          .to_broadcast([128, 8, 8, 8])
                i1a4 = I1A[:].rearrange("p (h c) -> p h c", c=8).unsqueeze(2)\
                           .to_broadcast([128, 8, 8, 8])
                i2a4 = I2A[:].rearrange("p (h c) -> p h c", c=8).unsqueeze(2)\
                           .to_broadcast([128, 8, 8, 8])
                i1sel = COLS[:, 96:160]
                i2sel = COLS[:, 0:64]
                fif = OHW[:, 0:64]
                T1 = OHW[:, 64:128]
                T2 = OHW[:, 128:192]
                SP = OHW[:, 448:512]
                ve.tensor_scalar(T1, PKA[:], -3.5, 0.125, op0=ALU.add, op1=ALU.mult)
                ve.tensor_scalar(fsrbuf[:], FSA[:], 0.0, None, op0=ALU.max)
                ve.tensor_scalar(T2, T1, C, None, op0=ALU.add)
                ve.tensor_copy(SP, PKA[:])
                ve.tensor_scalar(I1S[:], T2, -C, None, op0=ALU.add)
                ve.tensor_copy(SP, PKA[:])
                ve.tensor_copy(SP, PKA[:])
                ve.scalar_tensor_tensor(out=I2S[:], in0=I1S[:], scalar=-8.0,
                                        in1=PKA[:], op0=ALU.mult, op1=ALU.add)
                ve.tensor_tensor(oh4, af4, io4, op=ALU.is_equal)
                ve.tensor_mul(oh4, oh4, i1a4)
                ve.reduce_sum(i1sel.rearrange("p (h j) -> p h j", j=8), oh4, axis=AX.X)
                ve.tensor_tensor(oh4, bf4, io4, op=ALU.is_equal)
                ve.tensor_mul(oh4, oh4, i2a4)
                ve.reduce_sum(i2sel.rearrange("p (h j) -> p h j", j=8), oh4, axis=AX.X)
                ve.tensor_copy(SP, PKA[:])
                ve.tensor_copy(SP, PKA[:])
                ve.scalar_tensor_tensor(out=fif, in0=i1sel, scalar=256.0,
                                        in1=i2sel, op0=ALU.mult, op1=ALU.add)
                ve.tensor_copy(SP, PKA[:])
                ve.tensor_copy(SP, PKA[:])
                ve.tensor_copy(fibuf[:], fif).then_inc(SEM["VE"], 1)
            S.on("vector", wtail)
            ve_last = S.inc("VE", 1)
            if dbg and ti == 3:
                def wdump(sy, ve_last=ve_last):
                    W(sy, "VE", ve_last)
                    sy.dma_start(out=dbg_t["simd"][128:256, 0:64], in_=I1A[:]).then_inc(SEM["OUT"], 16)
                    sy.dma_start(out=dbg_t["simd"][128:256, 64:128], in_=I2A[:]).then_inc(SEM["OUT"], 16)
                    sy.dma_start(out=dbg_t["simd"][128:256, 128:192], in_=PKA[:]).then_inc(SEM["OUT"], 16)
                    sy.dma_start(out=dbg_t["simd"][128:256, 192:256], in_=I1S[:]).then_inc(SEM["OUT"], 16)
                    sy.dma_start(out=dbg_t["simd"][128:256, 256:320], in_=I2S[:]).then_inc(SEM["OUT"], 16)
                    sy.dma_start(out=dbg_t["simd"][128:256, 320:384], in_=FSA[:]).then_inc(SEM["OUT"], 16)
                    sy.dma_start(out=dbg_t["cod"][:], in_=COLS[:]).then_inc(SEM["OUT"], 16)
                S.on("sync", wdump)
                wd_out = S.inc("OUT", 16 * 7)
                def wgate(ve, wd_out=wd_out):
                    W(ve, "OUT", wd_out)
                S.on("vector", wgate)
            return ve_last, ve_sev_prev

        def ph_ffn_tile(ti, par, ve_route, xnbuf, y1buf, fibuf, fsrbuf, y2buf,
                        gat_free_pe, ve_psup_free, out_prev):
            hraw, gh = HRAW[par], GH[par]
            pe_up = []
            for g in range(NGRP):
                gb = GAT[g % 2]
                gfree = gat_free_pe[g % 2] if g < 2 else pe_up[g - 2]
                g_first = []
                for k in range(SGRP):
                    def gath(gp, g=g, k=k, gb=gb, gfree=gfree, ve_route=ve_route):
                        if g == 0 and k == 0:
                            W(gp, "VE", ve_route)
                        if k == 0 and gfree > 0:
                            W(gp, "PE", gfree)
                        j = g * SGRP + k
                        gp.indirect_dma_start(
                            out=gb[:, k * 2048:(k + 1) * 2048], out_offset=None,
                            in_=comb[:],
                            in_offset=IndirectOffsetOnAxis(fibuf[:, j:j + 1], 0),
                        ).then_inc(SEM["G"], 16)
                    S.on("gpsimd", gath)
                    S.inc("G", 16)
                g_cnt = S.get("G")

                def down(ve, g=g, gb=gb, g_cnt=g_cnt, xnbuf=xnbuf, hraw=hraw):
                    W(ve, "G", g_cnt)
                    li = None
                    for k in range(SGRP):
                        j = g * SGRP + k
                        ve.tensor_mul(U1[:], gb[:, k * 2048:k * 2048 + 1024],
                                      xnbuf[:])
                        li = ve.reduce_sum(hraw[:, j:j + 1], U1[:], axis=AX.X)
                    li.then_inc(SEM["VE"], 1)
                S.on("vector", down)
                ve_dn = S.inc("VE", 1)

                def act(sc, g=g, hraw=hraw, ve_dn=ve_dn):
                    W(sc, "VE", ve_dn)
                    sc.activation(out=COLS[:, 160:160 + SGRP],
                                  in_=hraw[:, g * SGRP:(g + 1) * SGRP],
                                  func=AF.Gelu).then_inc(SEM["SC"], 1)
                S.on("scalar", act)
                sc_g = S.inc("SC", 1)

                def ghm(ve, g=g, gh=gh, fsrbuf=fsrbuf, sc_g=sc_g):
                    W(ve, "SC", sc_g)
                    ve.tensor_mul(gh[:, g * SGRP:(g + 1) * SGRP],
                                  COLS[:, 160:160 + SGRP],
                                  fsrbuf[:, g * SGRP:(g + 1) * SGRP]).then_inc(SEM["VE"], 1)
                S.on("vector", ghm)
                ve_gh = S.inc("VE", 1)

                sb = SBL[g % 2]

                def sbuild(sc, g=g, gh=gh, sb=sb, ve_gh=ve_gh,
                           pe_gate=(pe_up[g - 2] if g >= 2 else 0)):
                    W(sc, "VE", ve_gh)
                    if pe_gate > 0:
                        W(sc, "PE", pe_gate)
                    li = None
                    for k in range(SGRP):
                        li = sc.mul(sb[:, k * 128:(k + 1) * 128], I128[:],
                                    gh[:, g * SGRP + k: g * SGRP + k + 1])
                    li.then_inc(SEM["SC"], 1)
                S.on("scalar", sbuild)
                sc_sb = S.inc("SC", 1)

                def up(te, g=g, sb=sb, gb=gb, sc_sb=sc_sb,
                       ve_psup=(ve_psup_free if g == 0 else 0)):
                    W(te, "SC", sc_sb)
                    if ve_psup > 0:
                        W(te, "VE", ve_psup)
                    li = None
                    for k in range(SGRP):
                        j = g * SGRP + k
                        first = (j == 0)
                        last = (j == NSLOT - 1)
                        li = te.matmul(ps_up[:, 0:512], sb[:, k * 128:(k + 1) * 128],
                                       gb[:, k * 2048 + 1024:k * 2048 + 1536],
                                       start=first, stop=last, skip_group_check=True)
                        li = te.matmul(ps_up[:, 512:1024], sb[:, k * 128:(k + 1) * 128],
                                       gb[:, k * 2048 + 1536:(k + 1) * 2048],
                                       start=first, stop=last, skip_group_check=True)
                    li.then_inc(SEM["PE"], 1)
                S.on("tensor", up)
                pe_up.append(S.inc("PE", 1))

            def ln2(ve, y1buf=y1buf, pe_done=pe_up[-1]):
                W(ve, "PE", pe_done)
                ve.tensor_add(U2[:], y1buf[:], ps_up[:])
                ve.reduce_sum(CS[:, 0:1], U2[:], axis=AX.X)
                ve.tensor_mul(U1[:], U2[:], U2[:])
                ve.reduce_sum(CS[:, 1:2], U1[:], axis=AX.X)
                ve.tensor_scalar(CS[:, 2:3], CS[:, 0:1], 1.0 / 1024, None, op0=ALU.mult)
                ve.tensor_scalar(CS[:, 3:4], CS[:, 1:2], 1.0 / 1024, None, op0=ALU.mult)
                ve.tensor_mul(CS[:, 4:5], CS[:, 2:3], CS[:, 2:3])
                ve.tensor_copy(CS[:, 15:16], CS[:, 0:1])
                ve.tensor_sub(CS[:, 5:6], CS[:, 3:4], CS[:, 4:5]).then_inc(SEM["VE"], 1)
            S.on("vector", ln2)
            ve_v2 = S.inc("VE", 1)

            def rst2(sc):
                W(sc, "VE", ve_v2)
                sc.activation(out=CS[:, 7:8], in_=CS[:, 5:6], func=AF.Sqrt,
                              bias=CS[:, 12:13], scale=1.0).then_inc(SEM["SC"], 1)
            S.on("scalar", rst2)
            sc_r2 = S.inc("SC", 1)

            def ln2b(ve, y2buf=y2buf, out_prev=out_prev):
                W(ve, "SC", sc_r2)
                if out_prev > 0:
                    W(ve, "OUT", out_prev)
                ve.reciprocal(CS[:, 6:7], CS[:, 7:8])
                ve.tensor_scalar(y2buf[:], U2[:], CS[:, 2:3], CS[:, 6:7],
                                 op0=ALU.subtract, op1=ALU.mult).then_inc(SEM["VE"], 1)
            S.on("vector", ln2b)
            ve_y2 = S.inc("VE", 1)

            def store(sy, ti=ti, y2buf=y2buf):
                W(sy, "VE", ve_y2)
                sy.dma_start(out=out[ti * 128:(ti + 1) * 128, :],
                             in_=y2buf[:]).then_inc(SEM["OUT"], 16)
            S.on("sync", store)
            out_cnt = S.inc("OUT", 16)

            if dbg:
                def dstore(sy, ti=ti, y1buf=y1buf, xnbuf=xnbuf, fibuf=fibuf,
                           fsrbuf=fsrbuf, hraw=hraw, ve_y2=ve_y2):
                    W(sy, "VE", ve_y2)
                    r = slice(ti * 128, (ti + 1) * 128)
                    sy.dma_start(out=dbg_t["y1d"][r, :], in_=y1buf[:]).then_inc(SEM["OUT"], 16)
                    sy.dma_start(out=dbg_t["fid"][r, :], in_=fibuf[:]).then_inc(SEM["OUT"], 16)
                    sy.dma_start(out=dbg_t["fsd"][r, :], in_=fsrbuf[:]).then_inc(SEM["OUT"], 16)
                    sy.dma_start(out=dbg_t["hrd"][r, :], in_=hraw[:]).then_inc(SEM["OUT"], 16)
                    sy.dma_start(out=dbg_t["xnd"][r, :], in_=xnbuf[:]).then_inc(SEM["OUT"], 16)
                    sy.dma_start(out=dbg_t["pbd"][r, :], in_=PB[ti % 2][:]).then_inc(SEM["OUT"], 16)
                S.on("sync", dstore)
                S.inc("OUT", 16 * 6)

            return (pe_up[-2], pe_up[-1]), ve_v2, out_cnt

        ve0 = ph_setup()
        prev_xfree = ve0
        psmm_gate = 0
        for pt in range(NPT):
            par = pt % 2
            ve_pb = ph_xload(x_pre, pt, par, prev_xfree, psmm_gate)
            ph_state_update(par, ve_pb, first=(pt == 0))
            prev_xfree = ve_pb

        gat_free = (0, 0)
        ve_psup_free = 0
        out_gate = [0, 0]
        prev_scxev = 0
        pend = None
        for ti in range(NT):
            par = ti % 2
            y1buf, xnbuf = Y1[ti % 3], XNB[ti % 3]
            xntbuf = XNT[par]
            fibuf, fsrbuf = FI[par], FSR[par]
            y2buf = Y2[par]
            ve_pb = ph_xload(x_own, ti, par, prev_xfree, psmm_gate, prev_scxev)
            ve_nud = ph_scan_tile(ti, par, ve_pb)
            ph_state_update(par, ve_nud, first=False)
            ve_xn, sc_xnt = ph_mixer_ln1(ti, par, ve_nud, y1buf, xnbuf, xntbuf)
            prev_xfree = ve_xn
            prev_scxev = sc_xnt
            if phase == "attn":
                def storea(sy, ti=ti, y1buf=y1buf, ve_xn=ve_xn, og=out_gate[par]):
                    W(sy, "VE", ve_xn)
                    if og > 0:
                        W(sy, "OUT", og)
                    sy.dma_start(out=out[ti * 128:(ti + 1) * 128, :],
                                 in_=y1buf[:]).then_inc(SEM["OUT"], 16)
                S.on("sync", storea)
                out_gate[par] = S.inc("OUT", 16)
                psmm_gate = ve_xn
                continue
            ve_route, ve_sims_last = ph_q_sims_routing(ti, par, sc_xnt, xntbuf,
                                                       fibuf, fsrbuf)
            psmm_gate = ve_sims_last
            if dbg and ti == 3:
                def fdump(sy, fibuf=fibuf, fsrbuf=fsrbuf, ve_route=ve_route):
                    W(sy, "VE", ve_route)
                    sy.dma_start(out=dbg_t["fid"][0:128, :], in_=fibuf[:]).then_inc(SEM["OUT"], 16)
                    sy.dma_start(out=dbg_t["fsd"][0:128, :], in_=fsrbuf[:]).then_inc(SEM["OUT"], 16)
                S.on("sync", fdump)
                S.inc("OUT", 32)
            if phase == "route":
                def storea(sy, ti=ti, y1buf=y1buf, fibuf=fibuf, fsrbuf=fsrbuf,
                           ve_route=ve_route, og=out_gate[par]):
                    W(sy, "VE", ve_route)
                    if og > 0:
                        W(sy, "OUT", og)
                    sy.dma_start(out=out[ti * 128:(ti + 1) * 128, :],
                                 in_=y1buf[:]).then_inc(SEM["OUT"], 16)
                S.on("sync", storea)
                out_gate[par] = S.inc("OUT", 16)
                continue
            if pend is not None:
                gat_free, ve_psup_free, oc = ph_ffn_tile(*pend[:-1],
                                                         gat_free, ve_psup_free,
                                                         out_gate[pend[-1]])
                out_gate[pend[-1]] = oc
            pend = (ti, par, ve_route, xnbuf, y1buf, fibuf, fsrbuf, y2buf, par)

        if phase == "full" and pend is not None:
            gat_free, ve_psup_free, oc = ph_ffn_tile(*pend[:-1],
                                                     gat_free, ve_psup_free,
                                                     out_gate[pend[-1]])
            out_gate[pend[-1]] = oc

        out_total = S.get("OUT")

        def fin(sy):
            sy.wait_ge(SEM["OUT"], out_total)
        S.on("sync", fin)

        block = E(nc.Block())

        @block.sync
        def _(sync):
            for fn in S.ops["sync"]:
                fn(sync)

        @block.vector
        def _(vector):
            for fn in S.ops["vector"]:
                fn(vector)

        @block.scalar
        def _(scalar):
            for fn in S.ops["scalar"]:
                fn(scalar)

        @block.tensor
        def _(tensor):
            for fn in S.ops["tensor"]:
                fn(tensor)

        @block.gpsimd
        def _(gpsimd):
            for fn in S.ops["gpsimd"]:
                fn(gpsimd)

    return nc


_NC_CACHE = {}


def _get_nc(dbg=False):
    if dbg not in _NC_CACHE:
        _NC_CACHE[dbg] = build_program(dbg)
    return _NC_CACHE[dbg]


def make_in_maps(L0_attn_in_sequence_bwc, L0_attn_W_proj, L0_attn_pre_metric_nkk,
                 L0_attn_transforms_nkk, L0_attn_W_mixer, L0_ffn_w_q, L0_ffn_keys,
                 L0_ffn_expert_down, L0_ffn_expert_up, L0_ffn_norm_w):
    x = np.ascontiguousarray(np.asarray(L0_attn_in_sequence_bwc, dtype=np.float32))
    wpT = np.ascontiguousarray(np.asarray(L0_attn_W_proj, np.float32).T)
    wmT = np.ascontiguousarray(np.asarray(L0_attn_W_mixer, np.float32).T)
    wqT = np.ascontiguousarray(np.asarray(L0_ffn_w_q, np.float32).T)
    keys = np.asarray(L0_ffn_keys, np.float32)
    keysT = np.ascontiguousarray(
        keys.transpose(0, 2, 3, 1).reshape(16, 128, 256))
    pm = np.asarray(L0_attn_pre_metric_nkk, np.float32)
    pmT = np.ascontiguousarray(pm.transpose(0, 2, 1))
    trfa = np.ascontiguousarray(np.asarray(L0_attn_transforms_nkk, np.float32))
    nw = np.ascontiguousarray(np.asarray(L0_ffn_norm_w, np.float32).reshape(1, 1024))
    comb = np.ascontiguousarray(np.concatenate(
        [np.asarray(L0_ffn_expert_down, np.float32),
         np.asarray(L0_ffn_expert_up, np.float32)], axis=1))
    zeros_pre = np.zeros((2048, 1024), np.float32)

    in_maps = []
    for c in range(8):
        b, h2 = c // 2, c % 2
        x_ownc = np.ascontiguousarray(x[b, h2 * 2048:(h2 + 1) * 2048, :])
        x_prec = np.ascontiguousarray(x[b, 0:2048, :]) if h2 == 1 else zeros_pre
        in_maps.append(dict(x_own=x_ownc, x_pre=x_prec, wpT=wpT, wmT=wmT,
                            wqT=wqT, keysT=keysT, pmT=pmT, trf=trfa, nw_in=nw,
                            comb=comb))
    return in_maps


def kernel(**inputs):
    in_maps = make_in_maps(**inputs)
    nc = _get_nc(False)
    res = run_bass_kernel_spmd(nc, in_maps, core_ids=list(range(8)))
    outp = np.empty((4, 4096, 1024), np.float32)
    for c in range(8):
        b, h2 = c // 2, c % 2
        outp[b, h2 * 2048:(h2 + 1) * 2048, :] = res.results[c]["out"]
    return outp


# version 03 (final)
# speedup vs baseline: 1.1249x; 1.0931x over previous
import numpy as np
from contextlib import ExitStack

import concourse.bass as bass
import concourse.mybir as mybir
from concourse.bass_utils import run_bass_kernel_spmd
from concourse.bass_core_rust import IndirectOffsetOnAxis

F32 = mybir.dt.float32
BF16 = mybir.dt.bfloat16
I32 = mybir.dt.int32
U16 = mybir.dt.uint16
U32 = mybir.dt.uint32
AF = mybir.ActivationFunctionType
ALU = mybir.AluOpType
AX = mybir.AxisListType

NT = 16
NPT = 16
NSLOT = 64
SGRP = 4
NGRP = NSLOT // SGRP


class Sched:
    ENGINES = ("sync", "vector", "scalar", "tensor", "gpsimd")

    def __init__(self):
        self.ops = {e: [] for e in self.ENGINES}
        self.cnt = {}

    def inc(self, sem, by):
        self.cnt[sem] = self.cnt.get(sem, 0) + by
        return self.cnt[sem]

    def get(self, sem):
        return self.cnt.get(sem, 0)

    def on(self, eng, fn):
        self.ops[eng].append(fn)


def build_program(dbg=False, phase="full"):
    nc = bass.Bass("TRN2", target_bir_lowering=False)

    x_own = nc.dram_tensor("x_own", [2048, 1024], F32, kind="ExternalInput")
    x_pre = nc.dram_tensor("x_pre", [2048, 1024], F32, kind="ExternalInput")
    wpT = nc.dram_tensor("wpT", [1024, 1024], F32, kind="ExternalInput")
    wmT = nc.dram_tensor("wmT", [1024, 1024], F32, kind="ExternalInput")
    wqT = nc.dram_tensor("wqT", [1024, 2048], F32, kind="ExternalInput")
    keysT = nc.dram_tensor("keysT", [16, 128, 256], F32, kind="ExternalInput")
    pmT = nc.dram_tensor("pmT", [16, 64, 64], F32, kind="ExternalInput")
    trf = nc.dram_tensor("trf", [16, 64, 64], F32, kind="ExternalInput")
    nw_in = nc.dram_tensor("nw_in", [1, 1024], F32, kind="ExternalInput")
    comb = nc.dram_tensor("comb", [65536, 2048], F32, kind="ExternalInput")
    out = nc.dram_tensor("out", [2048, 1024], F32, kind="ExternalOutput")
    dbg_t = {}
    if dbg:
        for nm, sh in (("y1d", [2048, 1024]), ("fid", [2048, 64]),
                       ("fsd", [2048, 64]), ("hrd", [2048, 64]),
                       ("xnd", [2048, 1024]), ("pbd", [2048, 1024])):
            dt_ = {"fid": U32, "xnd": BF16, "pbd": BF16}.get(nm, F32)
            dbg_t[nm] = nc.dram_tensor(nm, sh, dt_, kind="ExternalOutput")
        dbg_t["simd"] = nc.dram_tensor("simd", [2048, 512], F32, kind="ExternalOutput")
        dbg_t["csd"] = nc.dram_tensor("csd", [2048, 16], F32, kind="ExternalOutput")
        dbg_t["cod"] = nc.dram_tensor("cod", [128, 192], F32, kind="ExternalOutput")
        dbg_t["cud"] = nc.dram_tensor("cud", [128, 24], U16, kind="ExternalOutput")
        dbg_t["cid"] = nc.dram_tensor("cid", [128, 24], I32, kind="ExternalOutput")
        dbg_t["nwd"] = nc.dram_tensor("nwd", [128, 1024], F32, kind="ExternalOutput")
        dbg_t["qtd"] = nc.dram_tensor("qtd", [2048, 256], BF16, kind="ExternalOutput")

    with ExitStack() as ctx:
        E = ctx.enter_context
        WPT = E(nc.sbuf_tensor("WPT", [128, 8192], BF16))
        WQT = E(nc.sbuf_tensor("WQT", [128, 16384], BF16))
        WMT = E(nc.sbuf_tensor("WMT", [128, 8192], BF16))
        KEYS = E(nc.sbuf_tensor("KEYS", [128, 4096], BF16))
        PMTS = E(nc.sbuf_tensor("PMTS", [128, 1024], BF16))
        MSB = E(nc.sbuf_tensor("MSB", [128, 1024], BF16))
        TRFS = E(nc.sbuf_tensor("TRFS", [128, 1024], BF16))
        NWR = E(nc.sbuf_tensor("NWR", [128, 1024], F32))
        IOTAF = E(nc.sbuf_tensor("IOTAF", [128, 128], F32))
        ROWID = E(nc.sbuf_tensor("ROWID", [128, 1], F32))
        IOTA8 = E(nc.sbuf_tensor("IOTA8", [128, 8], F32))
        I128 = E(nc.sbuf_tensor("I128", [128, 128], BF16))
        TRIL = E(nc.sbuf_tensor("TRIL", [128, 128], BF16))

        XF32 = [E(nc.sbuf_tensor(f"XF32_{i}", [128, 1024], F32)) for i in range(2)]
        XB = [E(nc.sbuf_tensor(f"XB_{i}", [128, 1024], BF16)) for i in range(2)]
        XTC = [E(nc.sbuf_tensor(f"XTC_{i}", [128, 1024], BF16)) for i in range(2)]
        PB = [E(nc.sbuf_tensor(f"PB_{i}", [128, 1024], BF16)) for i in range(2)]
        PTC = [E(nc.sbuf_tensor(f"PTC_{i}", [128, 2048], BF16)) for i in range(2)]
        QMSB = E(nc.sbuf_tensor("QMSB", [128, 256], BF16))
        SSB = E(nc.sbuf_tensor("SSB", [128, 1024], BF16))
        AMSK = E(nc.sbuf_tensor("AMSK", [128, 256], BF16))
        OTSB = E(nc.sbuf_tensor("OTSB", [128, 256], BF16))
        NUD = E(nc.sbuf_tensor("NUD", [128, 1024], BF16))
        U1 = E(nc.sbuf_tensor("U1", [128, 1024], F32))
        U2 = E(nc.sbuf_tensor("U2", [128, 1024], F32))
        Y1 = [E(nc.sbuf_tensor(f"Y1_{i}", [128, 1024], F32)) for i in range(3)]
        XNB = [E(nc.sbuf_tensor(f"XNB_{i}", [128, 1024], BF16)) for i in range(3)]
        XNT = [E(nc.sbuf_tensor(f"XNT_{i}", [128, 1024], BF16)) for i in range(2)]
        QTT = [E(nc.sbuf_tensor("QTT_0", [128, 2048], BF16))]
        SIMS = E(nc.sbuf_tensor("SIMS", [128, 512], F32))
        COLS = E(nc.sbuf_tensor("COLS", [128, 192], F32))
        COLI = E(nc.sbuf_tensor("COLI", [128, 24], I32))
        COLU = E(nc.sbuf_tensor("COLU", [128, 24], U16))
        OH = E(nc.sbuf_tensor("OH", [128, 64], F32))
        FI = [E(nc.sbuf_tensor(f"FI_{i}", [128, 64], U32)) for i in range(2)]
        FSR = [E(nc.sbuf_tensor(f"FSR_{i}", [128, 64], F32)) for i in range(2)]
        HRAW = [E(nc.sbuf_tensor(f"HRAW_{i}", [128, 64], F32)) for i in range(2)]
        GH = [E(nc.sbuf_tensor(f"GH_{i}", [128, 64], F32)) for i in range(2)]
        GAT = [E(nc.sbuf_tensor(f"GAT_{i}", [128, SGRP * 2048], BF16))
               for i in range(2)]
        SBL = [E(nc.sbuf_tensor(f"SBL_{i}", [128, SGRP * 128], BF16))
               for i in range(2)]
        I1A = E(nc.sbuf_tensor("I1A", [128, 64], F32))
        I2A = E(nc.sbuf_tensor("I2A", [128, 64], F32))
        PKA = E(nc.sbuf_tensor("PKA", [128, 64], F32))
        FSA = E(nc.sbuf_tensor("FSA", [128, 64], F32))
        OHW = E(nc.sbuf_tensor("OHW", [128, 512], F32))
        I1S = E(nc.sbuf_tensor("I1S", [128, 64], F32))
        I2S = E(nc.sbuf_tensor("I2S", [128, 64], F32))
        Y2 = [E(nc.sbuf_tensor(f"Y2_{i}", [128, 1024], F32)) for i in range(2)]
        CS = E(nc.sbuf_tensor("CS", [128, 16], F32))

        ps_mm = E(nc.psum_tensor("ps_mm", [128, 1024], F32))
        ps_up = E(nc.psum_tensor("ps_up", [128, 1024], F32))
        ps_state = E(nc.psum_tensor("ps_state", [64, 1024], F32))
        ps_scan = E(nc.psum_tensor("ps_scan", [128, 512], F32))
        ps_tr = E(nc.psum_tensor("ps_tr", [128, 1024], BF16))

        sem_names = ["LX", "LW", "G", "OUT", "PE", "VE", "SC", "GP"]
        SEM = {n: E(nc.semaphore(n)) for n in sem_names}
        S = Sched()

        def W(eng, sem, val):
            if val > 0:
                eng.wait_ge(SEM[sem], val)

        def ph_setup():
            def g_load(gp):
                gp.dma_start(out=WPT[:].rearrange("p (dc j) -> p dc j", j=1024),
                             in_=wpT[:].rearrange("(dc p) j -> p dc j", p=128)).then_inc(SEM["LW"], 16)
                gp.dma_start(out=WMT[:].rearrange("p (dc j) -> p dc j", j=1024),
                             in_=wmT[:].rearrange("(dc p) j -> p dc j", p=128)).then_inc(SEM["LW"], 16)
                gp.dma_start(out=WQT[:].rearrange("p (dc j) -> p dc j", j=2048),
                             in_=wqT[:].rearrange("(dc p) j -> p dc j", p=128)).then_inc(SEM["LW"], 16)
                gp.dma_start(out=KEYS[:].rearrange("p (c n) -> p c n", n=256),
                             in_=keysT[:].rearrange("c p n -> p c n")).then_inc(SEM["LW"], 16)
                gp.dma_start(out=PMTS[0:64, :].rearrange("p (h i) -> p h i", i=64),
                             in_=pmT[:].rearrange("h p i -> p h i")).then_inc(SEM["LW"], 16)
                gp.dma_start(out=TRFS[0:64, :].rearrange("p (h j) -> p h j", j=64),
                             in_=trf[:].rearrange("h p j -> p h j")).then_inc(SEM["LW"], 16)
                gp.dma_start(out=NWR[:],
                             in_=bass.AP(nw_in, 0, [[0, 128], [1, 1024]])).then_inc(SEM["LW"], 16)
                gp.iota(IOTAF[:], [[1, 128]], channel_multiplier=0,
                        allow_small_or_imprecise_dtypes=True)
                gp.iota(ROWID[:], [[0, 1]], channel_multiplier=1,
                        allow_small_or_imprecise_dtypes=True)
                gp.iota(IOTA8[:], [[1, 8]], channel_multiplier=0,
                        allow_small_or_imprecise_dtypes=True).then_inc(SEM["GP"], 1)
            S.on("gpsimd", g_load)
            lw_cnt = S.inc("LW", 16 * 7)
            gp_cnt = S.inc("GP", 1)

            def v_consts(ve):
                W(ve, "GP", gp_cnt)
                ve.tensor_scalar(I128[:], IOTAF[:], ROWID[:, 0:1], None, op0=ALU.is_equal)
                ve.tensor_scalar(TRIL[:], IOTAF[:], ROWID[:, 0:1], None, op0=ALU.is_ge)
                ve.memset(CS[:, 12:13], 1e-5)
                ve.memset(CS[:, 13:14], 1e-6).then_inc(SEM["VE"], 1)
            S.on("vector", v_consts)
            ve_c = S.inc("VE", 1)

            def t_setup(te):
                W(te, "LW", lw_cnt)
                W(te, "VE", ve_c)
                for h in range(16):
                    te.matmul(ps_state[0:64, h * 64:(h + 1) * 64],
                              PMTS[0:64, h * 64:(h + 1) * 64],
                              PMTS[0:64, h * 64:(h + 1) * 64],
                              start=True, stop=True)
                te.matmul(ps_scan[0:64, 0:64], PMTS[0:64, 0:64], PMTS[0:64, 0:64],
                          start=True, stop=True).then_inc(SEM["PE"], 1)
            S.on("tensor", t_setup)
            pe_c = S.inc("PE", 1)

            def v_setup2(ve):
                W(ve, "PE", pe_c)
                ve.tensor_copy(MSB[0:64, :], ps_state[0:64, :]).then_inc(SEM["VE"], 1)
            S.on("vector", v_setup2)
            return S.inc("VE", 1)

        def ph_xload(src, ti, par, wait_ve_xbuf, wait_ve_psmm, wait_sc_pstr=0):
            def ld(sy):
                W(sy, "VE", wait_ve_xbuf)
                sy.dma_start(out=XF32[par][:],
                             in_=src[ti * 128:(ti + 1) * 128, :]).then_inc(SEM["LX"], 16)
            S.on("sync", ld)
            lx = S.inc("LX", 16)

            def cast(ve):
                W(ve, "LX", lx)
                ve.tensor_copy(XB[par][:], XF32[par][:]).then_inc(SEM["VE"], 1)
            S.on("vector", cast)
            ve_cast = S.inc("VE", 1)

            def tr(te):
                W(te, "VE", ve_cast)
                if wait_sc_pstr > 0:
                    W(te, "SC", wait_sc_pstr)
                li = None
                for dc in range(8):
                    li = te.transpose(ps_tr[:, dc * 128:(dc + 1) * 128],
                                      XB[par][:, dc * 128:(dc + 1) * 128], I128[:])
                li.then_inc(SEM["PE"], 1)
            S.on("tensor", tr)
            pe_tr = S.inc("PE", 1)

            def ev(sc):
                W(sc, "PE", pe_tr)
                sc.copy(XTC[par][:], ps_tr[:]).then_inc(SEM["SC"], 1)
            S.on("scalar", ev)
            sc_ev = S.inc("SC", 1)

            def proj(te):
                W(te, "SC", sc_ev)
                W(te, "VE", wait_ve_psmm)
                li = None
                for half in range(2):
                    for dc in range(8):
                        li = te.matmul(
                            ps_mm[:, half * 512:(half + 1) * 512],
                            XTC[par][:, dc * 128:(dc + 1) * 128],
                            WPT[:, dc * 1024 + half * 512: dc * 1024 + (half + 1) * 512],
                            start=(dc == 0), stop=(dc == 7))
                li.then_inc(SEM["PE"], 1)
            S.on("tensor", proj)
            pe_proj = S.inc("PE", 1)

            def pb(ve):
                W(ve, "PE", pe_proj)
                ve.tensor_copy(PB[par][:], ps_mm[:]).then_inc(SEM["VE"], 1)
            S.on("vector", pb)
            ve_pb = S.inc("VE", 1)
            return ve_pb

        def ph_state_update(par, wait_ve, first):
            def st(te):
                W(te, "VE", wait_ve)
                li = None
                for h in range(16):
                    li = te.matmul(ps_state[0:64, h * 64:(h + 1) * 64],
                                   PB[par][:, h * 64:(h + 1) * 64],
                                   PB[par][:, h * 64:(h + 1) * 64],
                                   start=(first and h in (0, 8)), stop=False,
                                   skip_group_check=True)
                li.then_inc(SEM["PE"], 1)
            S.on("tensor", st)
            return S.inc("PE", 1)

        def ph_scan_tile(ti, par, ve_pb):
            def ptr_a(te):
                W(te, "VE", ve_pb)
                li = None
                for h in range(8):
                    li = te.transpose(ps_tr[0:64, h * 128:(h + 1) * 128],
                                      PB[par][:, h * 64:(h + 1) * 64], I128[:])
                li.then_inc(SEM["PE"], 1)
            S.on("tensor", ptr_a)
            pe_ptr_a = S.inc("PE", 1)

            def ptev_a(sc):
                W(sc, "PE", pe_ptr_a)
                sc.copy(PTC[par][0:64, 0:1024], ps_tr[0:64, 0:1024]).then_inc(SEM["SC"], 1)
            S.on("scalar", ptev_a)
            sc_pt_a = S.inc("SC", 1)

            def ptr_b(te):
                W(te, "SC", sc_pt_a)
                li = None
                for h in range(8, 16):
                    li = te.transpose(ps_tr[0:64, (h - 8) * 128:(h - 7) * 128],
                                      PB[par][:, h * 64:(h + 1) * 64], I128[:])
                li.then_inc(SEM["PE"], 1)
            S.on("tensor", ptr_b)
            pe_ptr_b = S.inc("PE", 1)

            def ptev_b(sc):
                W(sc, "PE", pe_ptr_b)
                sc.copy(PTC[par][0:64, 1024:2048], ps_tr[0:64, 0:1024]).then_inc(SEM["SC"], 1)
            S.on("scalar", ptev_b)
            sc_pt = S.inc("SC", 1)

            def ssnap(ve):
                ve.tensor_copy(SSB[0:64, :], ps_state[0:64, :]).then_inc(SEM["VE"], 1)
            S.on("vector", ssnap)
            ve_snap = S.inc("VE", 1)

            last_ve = ve_snap
            ve_e3 = 0
            for h in range(16):
                hp = h % 2

                def s1(te, h=h, first=(h == 0), sc_pt=sc_pt, ve_snap=ve_snap):
                    if first:
                        W(te, "SC", sc_pt)
                        W(te, "VE", ve_snap)
                    te.matmul(ps_scan[0:64, 128:256],
                              MSB[0:64, h * 64:(h + 1) * 64],
                              PTC[par][0:64, h * 128:(h + 1) * 128],
                              start=True, stop=True).then_inc(SEM["PE"], 1)
                S.on("tensor", s1)
                pe_s1 = S.inc("PE", 1)

                def e1(ve, h=h, pe_s1=pe_s1):
                    W(ve, "PE", pe_s1)
                    ve.tensor_copy(QMSB[0:64, (h % 2) * 128:(h % 2 + 1) * 128],
                                   ps_scan[0:64, 128:256]).then_inc(SEM["VE"], 1)
                S.on("vector", e1)
                ve_e1 = S.inc("VE", 1)

                def s2(te, h=h, ve_e1=ve_e1):
                    W(te, "VE", ve_e1)
                    te.matmul(ps_scan[:, 0:128],
                              PTC[par][0:64, h * 128:(h + 1) * 128],
                              QMSB[0:64, (h % 2) * 128:(h % 2 + 1) * 128],
                              start=True, stop=True).then_inc(SEM["PE"], 1)
                S.on("tensor", s2)
                pe_s2 = S.inc("PE", 1)

                def msk(ve, hp=hp, pe_s2=pe_s2):
                    W(ve, "PE", pe_s2)
                    ve.tensor_mul(AMSK[:, hp * 128:(hp + 1) * 128],
                                  ps_scan[:, 0:128], TRIL[:]).then_inc(SEM["VE"], 1)
                S.on("vector", msk)
                ve_msk = S.inc("VE", 1)

                def s34(te, h=h, hp=hp, ve_msk=ve_msk):
                    W(te, "VE", ve_msk)
                    te.matmul(ps_scan[0:64, 256:384],
                              SSB[0:64, h * 64:(h + 1) * 64],
                              QMSB[0:64, (h % 2) * 128:(h % 2 + 1) * 128],
                              start=True, stop=False, skip_group_check=True)
                    te.matmul(ps_scan[0:64, 256:384],
                              PB[par][:, h * 64:(h + 1) * 64],
                              AMSK[:, hp * 128:(hp + 1) * 128],
                              start=False, stop=True,
                              skip_group_check=True).then_inc(SEM["PE"], 1)
                S.on("tensor", s34)
                pe_s34 = S.inc("PE", 1)

                def e2(sc, h=h, pe_s34=pe_s34):
                    W(sc, "PE", pe_s34)
                    sc.mul(OTSB[0:64, (h % 2) * 128:(h % 2 + 1) * 128],
                           ps_scan[0:64, 256:384], 0.125).then_inc(SEM["SC"], 1)
                S.on("scalar", e2)
                sc_e2 = S.inc("SC", 1)

                def s5(te, h=h, sc_e2=sc_e2, ve_e3=ve_e3):
                    W(te, "SC", sc_e2)
                    if h % 2 == 0 and ve_e3 > 0:
                        W(te, "VE", ve_e3)
                    tp = (0, 64) if h % 2 == 1 else (0, 0)
                    dst = (ps_scan[64:128, 384:512] if h % 2 == 1
                           else ps_scan[0:64, 384:512])
                    te.matmul(dst, TRFS[0:64, h * 64:(h + 1) * 64],
                              OTSB[0:64, (h % 2) * 128:(h % 2 + 1) * 128],
                              start=True, stop=True,
                              tile_position=tp).then_inc(SEM["PE"], 1)
                S.on("tensor", s5)
                pe_s5 = S.inc("PE", 1)

                if h % 2 == 1:
                    def e3(ve, h=h, pe_s5=pe_s5):
                        W(ve, "PE", pe_s5)
                        ve.tensor_copy(NUD[:, (h // 2) * 128:(h // 2 + 1) * 128],
                                       ps_scan[:, 384:512]).then_inc(SEM["VE"], 1)
                    S.on("vector", e3)
                    ve_e3 = S.inc("VE", 1)
                    last_ve = ve_e3
            return last_ve

        def ph_mixer_ln1(ti, par, ve_nud, y1buf, xnbuf, xntbuf):
            def mx(te):
                W(te, "VE", ve_nud)
                li = None
                for half in range(2):
                    for c in range(8):
                        li = te.matmul(
                            ps_mm[:, half * 512:(half + 1) * 512],
                            NUD[:, c * 128:(c + 1) * 128],
                            WMT[:, c * 1024 + half * 512: c * 1024 + (half + 1) * 512],
                            start=(c == 0), stop=(c == 7))
                li.then_inc(SEM["PE"], 1)
            S.on("tensor", mx)
            pe_mx = S.inc("PE", 1)

            def ln1(ve):
                W(ve, "PE", pe_mx)
                ve.tensor_add(U1[:], XF32[par][:], ps_mm[:])
                ve.reduce_sum(CS[:, 0:1], U1[:], axis=AX.X)
                ve.tensor_mul(U2[:], U1[:], U1[:])
                ve.reduce_sum(CS[:, 1:2], U2[:], axis=AX.X)
                ve.tensor_scalar(CS[:, 2:3], CS[:, 0:1], 1.0 / 1024, None, op0=ALU.mult)
                ve.tensor_scalar(CS[:, 3:4], CS[:, 1:2], 1.0 / 1024, None, op0=ALU.mult)
                ve.tensor_mul(CS[:, 4:5], CS[:, 2:3], CS[:, 2:3])
                ve.tensor_copy(CS[:, 15:16], CS[:, 0:1])
                ve.tensor_sub(CS[:, 5:6], CS[:, 3:4], CS[:, 4:5]).then_inc(SEM["VE"], 1)
            S.on("vector", ln1)
            ve_st = S.inc("VE", 1)

            def rst(sc):
                W(sc, "VE", ve_st)
                sc.activation(out=CS[:, 7:8], in_=CS[:, 5:6], func=AF.Sqrt,
                              bias=CS[:, 12:13], scale=1.0).then_inc(SEM["SC"], 1)
            S.on("scalar", rst)
            sc_rst = S.inc("SC", 1)

            def ln1b(ve):
                W(ve, "SC", sc_rst)
                ve.reciprocal(CS[:, 6:7], CS[:, 7:8])
                ve.tensor_scalar(y1buf[:], U1[:], CS[:, 2:3], CS[:, 6:7],
                                 op0=ALU.subtract, op1=ALU.mult)
                ve.tensor_mul(U2[:], y1buf[:], y1buf[:])
                ve.reduce_sum(CS[:, 8:9], U2[:], axis=AX.X)
                ve.tensor_copy(CS[:, 15:16], CS[:, 0:1])
                ve.tensor_scalar(CS[:, 9:10], CS[:, 8:9], 1.0 / 1024, None,
                                 op0=ALU.mult).then_inc(SEM["VE"], 1)
            S.on("vector", ln1b)
            ve_ms = S.inc("VE", 1)

            def rin(sc):
                W(sc, "VE", ve_ms)
                sc.activation(out=CS[:, 11:12], in_=CS[:, 9:10], func=AF.Sqrt,
                              bias=CS[:, 13:14], scale=1.0).then_inc(SEM["SC"], 1)
            S.on("scalar", rin)
            sc_rin = S.inc("SC", 1)

            def xn(ve):
                W(ve, "SC", sc_rin)
                ve.reciprocal(CS[:, 10:11], CS[:, 11:12])
                ve.tensor_copy(CS[:, 15:16], CS[:, 0:1])
                ve.tensor_scalar(U2[:], y1buf[:], CS[:, 10:11], None, op0=ALU.mult)
                ve.tensor_mul(xnbuf[:], U2[:], NWR[:]).then_inc(SEM["VE"], 1)
            S.on("vector", xn)
            ve_xn = S.inc("VE", 1)

            if dbg and ti == 3:
                def xdump(sy, xnbuf=xnbuf, y1buf=y1buf, ve_xn=ve_xn):
                    W(sy, "VE", ve_xn)
                    sy.dma_start(out=dbg_t["xnd"][0:128, :], in_=xnbuf[:]).then_inc(SEM["OUT"], 16)
                    sy.dma_start(out=dbg_t["y1d"][0:128, :], in_=y1buf[:]).then_inc(SEM["OUT"], 16)
                    sy.dma_start(out=dbg_t["csd"][0:128, :], in_=CS[:]).then_inc(SEM["OUT"], 16)
                    sy.dma_start(out=dbg_t["nwd"][:], in_=NWR[:]).then_inc(SEM["OUT"], 16)
                S.on("sync", xdump)
                dump_out = S.inc("OUT", 64)
            else:
                dump_out = 0

            def xtr(te):
                if dump_out > 0:
                    W(te, "OUT", dump_out)
                W(te, "VE", ve_xn)
                li = None
                for dc in range(8):
                    li = te.transpose(ps_tr[:, dc * 128:(dc + 1) * 128],
                                      xnbuf[:, dc * 128:(dc + 1) * 128], I128[:])
                li.then_inc(SEM["PE"], 1)
            S.on("tensor", xtr)
            pe_xtr = S.inc("PE", 1)

            def xev(sc):
                W(sc, "PE", pe_xtr)
                sc.copy(xntbuf[:], ps_tr[:]).then_inc(SEM["SC"], 1)
            S.on("scalar", xev)
            sc_xev = S.inc("SC", 1)
            return ve_xn, sc_xev

        def ph_q_sims_routing(ti, par, sc_xnt, xntbuf, fibuf, fsrbuf):
            qt = QTT[0]
            ve_qprev = 0
            for grp in range(2):
                def qmm(te, grp=grp, sc_xnt=sc_xnt, ve_qprev=ve_qprev):
                    if grp == 0:
                        W(te, "SC", sc_xnt)
                    else:
                        W(te, "VE", ve_qprev)
                    li = None
                    for qs in range(8):
                        qq = grp * 8 + qs
                        for dc in range(8):
                            li = te.matmul(
                                ps_mm[:, qs * 128:(qs + 1) * 128],
                                WQT[:, dc * 2048 + qq * 128: dc * 2048 + (qq + 1) * 128],
                                xntbuf[:, dc * 128:(dc + 1) * 128],
                                start=(dc == 0), stop=(dc == 7))
                    li.then_inc(SEM["PE"], 1)
                S.on("tensor", qmm)
                pe_q = S.inc("PE", 1)

                def qev(ve, grp=grp, pe_q=pe_q):
                    W(ve, "PE", pe_q)
                    ve.tensor_copy(qt[:, grp * 1024:(grp + 1) * 1024],
                                   ps_mm[:]).then_inc(SEM["VE"], 1)
                S.on("vector", qev)
                ve_qprev = S.inc("VE", 1)
            ve_qdone = ve_qprev

            ve_last = ve_qdone
            ve_sev_prev = 0
            for h in range(8):
                def smm(te, h=h, ve_qdone=ve_qdone, ve_sev_prev=ve_sev_prev):
                    if h == 0:
                        W(te, "VE", ve_qdone)
                    else:
                        W(te, "VE", ve_sev_prev)
                    li = None
                    for which in range(2):
                        qs = h * 2 + which
                        li = te.matmul(ps_mm[:, which * 256:(which + 1) * 256],
                                       qt[:, qs * 128:(qs + 1) * 128],
                                       KEYS[:, qs * 256:(qs + 1) * 256],
                                       start=True, stop=True)
                    li.then_inc(SEM["PE"], 1)
                S.on("tensor", smm)
                pe_sm = S.inc("PE", 1)

                def sev(ve, pe_sm=pe_sm):
                    W(ve, "PE", pe_sm)
                    ve.tensor_copy(SIMS[:], ps_mm[:, 0:512]).then_inc(SEM["VE"], 1)
                S.on("vector", sev)
                ve_sev_prev = S.inc("VE", 1)
                sdump_out = 0
                if dbg and h == 0 and ti == 3:
                    def sdump(sy, ve_sev=ve_sev_prev):
                        W(sy, "VE", ve_sev)
                        sy.dma_start(out=dbg_t["simd"][0:128, :],
                                     in_=SIMS[:]).then_inc(SEM["OUT"], 16)
                        sy.dma_start(out=dbg_t["qtd"][0:128, :],
                                     in_=QTT[0][:, 0:256]).then_inc(SEM["OUT"], 16)
                    S.on("sync", sdump)
                    sdump_out = S.inc("OUT", 32)

                def route(ve, h=h, sdump_out=sdump_out):
                    if sdump_out > 0:
                        W(ve, "OUT", sdump_out)
                    s1t, s2t = COLS[:, 0:8], COLS[:, 8:16]
                    i1u, i2u, pku = COLU[:, 0:8], COLU[:, 8:16], COLU[:, 16:24]
                    ps64 = COLS[:, 96:160]
                    ve.max(out=s1t, in_=SIMS[:, 0:256])
                    ve.tensor_copy(COLS[:, 80:88], s1t)
                    ve.max_index(out=i1u, in_max=s1t, in_values=SIMS[:, 0:256])
                    ve.max(out=s2t, in_=SIMS[:, 256:512])
                    ve.tensor_copy(COLS[:, 80:88], s2t)
                    ve.max_index(out=i2u, in_max=s2t, in_values=SIMS[:, 256:512])
                    ve.tensor_tensor(ps64.rearrange("p (a b) -> p a b", b=8),
                                     s1t.unsqueeze(2).to_broadcast([128, 8, 8]),
                                     s2t.unsqueeze(1).to_broadcast([128, 8, 8]),
                                     op=ALU.add)
                    ve.max(out=FSA[:, h * 8:(h + 1) * 8], in_=ps64)
                    ve.tensor_copy(COLS[:, 80:88], FSA[:, h * 8:(h + 1) * 8])
                    ve.max_index(out=pku, in_max=FSA[:, h * 8:(h + 1) * 8],
                                 in_values=ps64)
                    ve.tensor_copy(I1A[:, h * 8:(h + 1) * 8], i1u)
                    ve.tensor_copy(I2A[:, h * 8:(h + 1) * 8], i2u)
                    ve.tensor_copy(PKA[:, h * 8:(h + 1) * 8],
                                   pku).then_inc(SEM["VE"], 1)
                S.on("vector", route)
                ve_last = S.inc("VE", 1)
                if dbg and h == 0 and ti == 3:
                    def rdump(sy, ve_r=ve_last):
                        W(sy, "VE", ve_r)
                        sy.dma_start(out=dbg_t["cod"][:], in_=COLS[:]).then_inc(SEM["OUT"], 16)
                        sy.dma_start(out=dbg_t["cud"][:], in_=COLU[:]).then_inc(SEM["OUT"], 16)
                        sy.dma_start(out=dbg_t["cid"][:], in_=COLI[:]).then_inc(SEM["OUT"], 16)
                    S.on("sync", rdump)
                    rdump_out = S.inc("OUT", 48)
                else:
                    rdump_out = 0
                if rdump_out:
                    def rgate(ve, rdump_out=rdump_out):
                        W(ve, "OUT", rdump_out)
                    S.on("vector", rgate)

            def wtail(ve, fibuf=fibuf, fsrbuf=fsrbuf):
                C = 12582912.0
                oh4 = OHW[:].rearrange("p (h j c) -> p h j c", j=8, c=8)
                io4 = IOTA8[:].unsqueeze(1).unsqueeze(2).to_broadcast([128, 8, 8, 8])
                af4 = I1S[:].rearrange("p (h j) -> p h j", j=8).unsqueeze(3)\
                          .to_broadcast([128, 8, 8, 8])
                bf4 = I2S[:].rearrange("p (h j) -> p h j", j=8).unsqueeze(3)\
                          .to_broadcast([128, 8, 8, 8])
                i1a4 = I1A[:].rearrange("p (h c) -> p h c", c=8).unsqueeze(2)\
                           .to_broadcast([128, 8, 8, 8])
                i2a4 = I2A[:].rearrange("p (h c) -> p h c", c=8).unsqueeze(2)\
                           .to_broadcast([128, 8, 8, 8])
                i1sel = COLS[:, 96:160]
                i2sel = COLS[:, 0:64]
                fif = OHW[:, 0:64]
                T1 = OHW[:, 64:128]
                T2 = OHW[:, 128:192]
                SP = OHW[:, 448:512]
                ve.tensor_scalar(T1, PKA[:], -3.5, 0.125, op0=ALU.add, op1=ALU.mult)
                ve.tensor_scalar(fsrbuf[:], FSA[:], 0.0, None, op0=ALU.max)
                ve.tensor_scalar(T2, T1, C, None, op0=ALU.add)
                ve.tensor_copy(SP, PKA[:])
                ve.tensor_scalar(I1S[:], T2, -C, None, op0=ALU.add)
                ve.tensor_copy(SP, PKA[:])
                ve.tensor_copy(SP, PKA[:])
                ve.scalar_tensor_tensor(out=I2S[:], in0=I1S[:], scalar=-8.0,
                                        in1=PKA[:], op0=ALU.mult, op1=ALU.add)
                ve.tensor_tensor(oh4, af4, io4, op=ALU.is_equal)
                ve.tensor_mul(oh4, oh4, i1a4)
                ve.reduce_sum(i1sel.rearrange("p (h j) -> p h j", j=8), oh4, axis=AX.X)
                ve.tensor_tensor(oh4, bf4, io4, op=ALU.is_equal)
                ve.tensor_mul(oh4, oh4, i2a4)
                ve.reduce_sum(i2sel.rearrange("p (h j) -> p h j", j=8), oh4, axis=AX.X)
                ve.tensor_copy(SP, PKA[:])
                ve.tensor_copy(SP, PKA[:])
                ve.scalar_tensor_tensor(out=fif, in0=i1sel, scalar=256.0,
                                        in1=i2sel, op0=ALU.mult, op1=ALU.add)
                ve.tensor_copy(SP, PKA[:])
                ve.tensor_copy(SP, PKA[:])
                ve.tensor_copy(fibuf[:], fif).then_inc(SEM["VE"], 1)
            S.on("vector", wtail)
            ve_last = S.inc("VE", 1)
            if dbg and ti == 3:
                def wdump(sy, ve_last=ve_last):
                    W(sy, "VE", ve_last)
                    sy.dma_start(out=dbg_t["simd"][128:256, 0:64], in_=I1A[:]).then_inc(SEM["OUT"], 16)
                    sy.dma_start(out=dbg_t["simd"][128:256, 64:128], in_=I2A[:]).then_inc(SEM["OUT"], 16)
                    sy.dma_start(out=dbg_t["simd"][128:256, 128:192], in_=PKA[:]).then_inc(SEM["OUT"], 16)
                    sy.dma_start(out=dbg_t["simd"][128:256, 192:256], in_=I1S[:]).then_inc(SEM["OUT"], 16)
                    sy.dma_start(out=dbg_t["simd"][128:256, 256:320], in_=I2S[:]).then_inc(SEM["OUT"], 16)
                    sy.dma_start(out=dbg_t["simd"][128:256, 320:384], in_=FSA[:]).then_inc(SEM["OUT"], 16)
                    sy.dma_start(out=dbg_t["cod"][:], in_=COLS[:]).then_inc(SEM["OUT"], 16)
                S.on("sync", wdump)
                wd_out = S.inc("OUT", 16 * 7)
                def wgate(ve, wd_out=wd_out):
                    W(ve, "OUT", wd_out)
                S.on("vector", wgate)
            return ve_last, ve_sev_prev

        def ph_ffn_tile(ti, par, ve_route, xnbuf, y1buf, fibuf, fsrbuf, y2buf,
                        gat_free_pe, ve_psup_free, out_prev):
            hraw, gh = HRAW[par], GH[par]
            pe_up = []
            for g in range(NGRP):
                gb = GAT[g % 2]
                gfree = gat_free_pe[g % 2] if g < 2 else pe_up[g - 2]
                g_first = []
                for k in range(SGRP):
                    def gath(gp, g=g, k=k, gb=gb, gfree=gfree, ve_route=ve_route):
                        if g == 0 and k == 0:
                            W(gp, "VE", ve_route)
                        if k == 0 and gfree > 0:
                            W(gp, "PE", gfree)
                        j = g * SGRP + k
                        gp.indirect_dma_start(
                            out=gb[:, k * 2048:(k + 1) * 2048], out_offset=None,
                            in_=comb[:],
                            in_offset=IndirectOffsetOnAxis(fibuf[:, j:j + 1], 0),
                        ).then_inc(SEM["G"], 16)
                    S.on("gpsimd", gath)
                    S.inc("G", 16)
                g_cnt = S.get("G")

                def down(ve, g=g, gb=gb, g_cnt=g_cnt, xnbuf=xnbuf, hraw=hraw):
                    W(ve, "G", g_cnt)
                    for k in range(SGRP):
                        j = g * SGRP + k
                        ve.scalar_tensor_tensor(
                            out=U1[:], in0=gb[:, k * 2048:k * 2048 + 1024],
                            scalar=0.0, in1=xnbuf[:],
                            op0=ALU.bypass, op1=ALU.mult,
                            accum_out=hraw[:, j:j + 1])
                    ve.tensor_copy(OHW[:, 448:512],
                                   PKA[:]).then_inc(SEM["VE"], 1)
                S.on("vector", down)
                ve_dn = S.inc("VE", 1)

                def act(sc, g=g, hraw=hraw, ve_dn=ve_dn):
                    W(sc, "VE", ve_dn)
                    sc.activation(out=COLS[:, 160:160 + SGRP],
                                  in_=hraw[:, g * SGRP:(g + 1) * SGRP],
                                  func=AF.Gelu).then_inc(SEM["SC"], 1)
                S.on("scalar", act)
                sc_g = S.inc("SC", 1)

                def ghm(ve, g=g, gh=gh, fsrbuf=fsrbuf, sc_g=sc_g):
                    W(ve, "SC", sc_g)
                    ve.tensor_mul(gh[:, g * SGRP:(g + 1) * SGRP],
                                  COLS[:, 160:160 + SGRP],
                                  fsrbuf[:, g * SGRP:(g + 1) * SGRP]).then_inc(SEM["VE"], 1)
                S.on("vector", ghm)
                ve_gh = S.inc("VE", 1)

                sb = SBL[g % 2]

                def sbuild(sc, g=g, gh=gh, sb=sb, ve_gh=ve_gh,
                           pe_gate=(pe_up[g - 2] if g >= 2 else 0)):
                    W(sc, "VE", ve_gh)
                    if pe_gate > 0:
                        W(sc, "PE", pe_gate)
                    li = None
                    for k in range(SGRP):
                        li = sc.mul(sb[:, k * 128:(k + 1) * 128], I128[:],
                                    gh[:, g * SGRP + k: g * SGRP + k + 1])
                    li.then_inc(SEM["SC"], 1)
                S.on("scalar", sbuild)
                sc_sb = S.inc("SC", 1)

                def up(te, g=g, sb=sb, gb=gb, sc_sb=sc_sb,
                       ve_psup=(ve_psup_free if g == 0 else 0)):
                    W(te, "SC", sc_sb)
                    if ve_psup > 0:
                        W(te, "VE", ve_psup)
                    li = None
                    for k in range(SGRP):
                        j = g * SGRP + k
                        first = (j == 0)
                        last = (j == NSLOT - 1)
                        li = te.matmul(ps_up[:, 0:512], sb[:, k * 128:(k + 1) * 128],
                                       gb[:, k * 2048 + 1024:k * 2048 + 1536],
                                       start=first, stop=last, skip_group_check=True)
                        li = te.matmul(ps_up[:, 512:1024], sb[:, k * 128:(k + 1) * 128],
                                       gb[:, k * 2048 + 1536:(k + 1) * 2048],
                                       start=first, stop=last, skip_group_check=True)
                    li.then_inc(SEM["PE"], 1)
                S.on("tensor", up)
                pe_up.append(S.inc("PE", 1))

            def ln2(ve, y1buf=y1buf, pe_done=pe_up[-1]):
                W(ve, "PE", pe_done)
                ve.tensor_add(U2[:], y1buf[:], ps_up[:])
                ve.reduce_sum(CS[:, 0:1], U2[:], axis=AX.X)
                ve.tensor_mul(U1[:], U2[:], U2[:])
                ve.reduce_sum(CS[:, 1:2], U1[:], axis=AX.X)
                ve.tensor_scalar(CS[:, 2:3], CS[:, 0:1], 1.0 / 1024, None, op0=ALU.mult)
                ve.tensor_scalar(CS[:, 3:4], CS[:, 1:2], 1.0 / 1024, None, op0=ALU.mult)
                ve.tensor_mul(CS[:, 4:5], CS[:, 2:3], CS[:, 2:3])
                ve.tensor_copy(CS[:, 15:16], CS[:, 0:1])
                ve.tensor_sub(CS[:, 5:6], CS[:, 3:4], CS[:, 4:5]).then_inc(SEM["VE"], 1)
            S.on("vector", ln2)
            ve_v2 = S.inc("VE", 1)

            def rst2(sc):
                W(sc, "VE", ve_v2)
                sc.activation(out=CS[:, 7:8], in_=CS[:, 5:6], func=AF.Sqrt,
                              bias=CS[:, 12:13], scale=1.0).then_inc(SEM["SC"], 1)
            S.on("scalar", rst2)
            sc_r2 = S.inc("SC", 1)

            def ln2b(ve, y2buf=y2buf, out_prev=out_prev):
                W(ve, "SC", sc_r2)
                if out_prev > 0:
                    W(ve, "OUT", out_prev)
                ve.reciprocal(CS[:, 6:7], CS[:, 7:8])
                ve.tensor_scalar(y2buf[:], U2[:], CS[:, 2:3], CS[:, 6:7],
                                 op0=ALU.subtract, op1=ALU.mult).then_inc(SEM["VE"], 1)
            S.on("vector", ln2b)
            ve_y2 = S.inc("VE", 1)

            def store(sy, ti=ti, y2buf=y2buf):
                W(sy, "VE", ve_y2)
                sy.dma_start(out=out[ti * 128:(ti + 1) * 128, :],
                             in_=y2buf[:]).then_inc(SEM["OUT"], 16)
            S.on("sync", store)
            out_cnt = S.inc("OUT", 16)

            if dbg:
                def dstore(sy, ti=ti, y1buf=y1buf, xnbuf=xnbuf, fibuf=fibuf,
                           fsrbuf=fsrbuf, hraw=hraw, ve_y2=ve_y2):
                    W(sy, "VE", ve_y2)
                    r = slice(ti * 128, (ti + 1) * 128)
                    sy.dma_start(out=dbg_t["y1d"][r, :], in_=y1buf[:]).then_inc(SEM["OUT"], 16)
                    sy.dma_start(out=dbg_t["fid"][r, :], in_=fibuf[:]).then_inc(SEM["OUT"], 16)
                    sy.dma_start(out=dbg_t["fsd"][r, :], in_=fsrbuf[:]).then_inc(SEM["OUT"], 16)
                    sy.dma_start(out=dbg_t["hrd"][r, :], in_=hraw[:]).then_inc(SEM["OUT"], 16)
                    sy.dma_start(out=dbg_t["xnd"][r, :], in_=xnbuf[:]).then_inc(SEM["OUT"], 16)
                    sy.dma_start(out=dbg_t["pbd"][r, :], in_=PB[ti % 2][:]).then_inc(SEM["OUT"], 16)
                S.on("sync", dstore)
                S.inc("OUT", 16 * 6)

            return (pe_up[-2], pe_up[-1]), ve_v2, out_cnt

        ve0 = ph_setup()
        prev_xfree = ve0
        psmm_gate = 0
        for pt in range(NPT):
            par = pt % 2
            ve_pb = ph_xload(x_pre, pt, par, prev_xfree, psmm_gate)
            ph_state_update(par, ve_pb, first=(pt == 0))
            prev_xfree = ve_pb

        gat_free = (0, 0)
        ve_psup_free = 0
        out_gate = [0, 0]
        prev_scxev = 0
        pend = None
        for ti in range(NT):
            par = ti % 2
            y1buf, xnbuf = Y1[ti % 3], XNB[ti % 3]
            xntbuf = XNT[par]
            fibuf, fsrbuf = FI[par], FSR[par]
            y2buf = Y2[par]
            ve_pb = ph_xload(x_own, ti, par, prev_xfree, psmm_gate, prev_scxev)
            ve_nud = ph_scan_tile(ti, par, ve_pb)
            ph_state_update(par, ve_nud, first=False)
            ve_xn, sc_xnt = ph_mixer_ln1(ti, par, ve_nud, y1buf, xnbuf, xntbuf)
            prev_xfree = ve_xn
            prev_scxev = sc_xnt
            if phase == "attn":
                def storea(sy, ti=ti, y1buf=y1buf, ve_xn=ve_xn, og=out_gate[par]):
                    W(sy, "VE", ve_xn)
                    if og > 0:
                        W(sy, "OUT", og)
                    sy.dma_start(out=out[ti * 128:(ti + 1) * 128, :],
                                 in_=y1buf[:]).then_inc(SEM["OUT"], 16)
                S.on("sync", storea)
                out_gate[par] = S.inc("OUT", 16)
                psmm_gate = ve_xn
                continue
            ve_route, ve_sims_last = ph_q_sims_routing(ti, par, sc_xnt, xntbuf,
                                                       fibuf, fsrbuf)
            psmm_gate = ve_sims_last
            if dbg and ti == 3:
                def fdump(sy, fibuf=fibuf, fsrbuf=fsrbuf, ve_route=ve_route):
                    W(sy, "VE", ve_route)
                    sy.dma_start(out=dbg_t["fid"][0:128, :], in_=fibuf[:]).then_inc(SEM["OUT"], 16)
                    sy.dma_start(out=dbg_t["fsd"][0:128, :], in_=fsrbuf[:]).then_inc(SEM["OUT"], 16)
                S.on("sync", fdump)
                S.inc("OUT", 32)
            if phase == "route":
                def storea(sy, ti=ti, y1buf=y1buf, fibuf=fibuf, fsrbuf=fsrbuf,
                           ve_route=ve_route, og=out_gate[par]):
                    W(sy, "VE", ve_route)
                    if og > 0:
                        W(sy, "OUT", og)
                    sy.dma_start(out=out[ti * 128:(ti + 1) * 128, :],
                                 in_=y1buf[:]).then_inc(SEM["OUT"], 16)
                S.on("sync", storea)
                out_gate[par] = S.inc("OUT", 16)
                continue
            if pend is not None:
                gat_free, ve_psup_free, oc = ph_ffn_tile(*pend[:-1],
                                                         gat_free, ve_psup_free,
                                                         out_gate[pend[-1]])
                out_gate[pend[-1]] = oc
            pend = (ti, par, ve_route, xnbuf, y1buf, fibuf, fsrbuf, y2buf, par)

        if phase == "full" and pend is not None:
            gat_free, ve_psup_free, oc = ph_ffn_tile(*pend[:-1],
                                                     gat_free, ve_psup_free,
                                                     out_gate[pend[-1]])
            out_gate[pend[-1]] = oc

        out_total = S.get("OUT")

        def fin(sy):
            sy.wait_ge(SEM["OUT"], out_total)
        S.on("sync", fin)

        block = E(nc.Block())

        @block.sync
        def _(sync):
            for fn in S.ops["sync"]:
                fn(sync)

        @block.vector
        def _(vector):
            for fn in S.ops["vector"]:
                fn(vector)

        @block.scalar
        def _(scalar):
            for fn in S.ops["scalar"]:
                fn(scalar)

        @block.tensor
        def _(tensor):
            for fn in S.ops["tensor"]:
                fn(tensor)

        @block.gpsimd
        def _(gpsimd):
            for fn in S.ops["gpsimd"]:
                fn(gpsimd)

    return nc


_NC_CACHE = {}


def _get_nc(dbg=False):
    if dbg not in _NC_CACHE:
        _NC_CACHE[dbg] = build_program(dbg)
    return _NC_CACHE[dbg]


def make_in_maps(L0_attn_in_sequence_bwc, L0_attn_W_proj, L0_attn_pre_metric_nkk,
                 L0_attn_transforms_nkk, L0_attn_W_mixer, L0_ffn_w_q, L0_ffn_keys,
                 L0_ffn_expert_down, L0_ffn_expert_up, L0_ffn_norm_w):
    x = np.ascontiguousarray(np.asarray(L0_attn_in_sequence_bwc, dtype=np.float32))
    wpT = np.ascontiguousarray(np.asarray(L0_attn_W_proj, np.float32).T)
    wmT = np.ascontiguousarray(np.asarray(L0_attn_W_mixer, np.float32).T)
    wqT = np.ascontiguousarray(np.asarray(L0_ffn_w_q, np.float32).T)
    keys = np.asarray(L0_ffn_keys, np.float32)
    keysT = np.ascontiguousarray(
        keys.transpose(0, 2, 3, 1).reshape(16, 128, 256))
    pm = np.asarray(L0_attn_pre_metric_nkk, np.float32)
    pmT = np.ascontiguousarray(pm.transpose(0, 2, 1))
    trfa = np.ascontiguousarray(np.asarray(L0_attn_transforms_nkk, np.float32))
    nw = np.ascontiguousarray(np.asarray(L0_ffn_norm_w, np.float32).reshape(1, 1024))
    comb = np.ascontiguousarray(np.concatenate(
        [np.asarray(L0_ffn_expert_down, np.float32),
         np.asarray(L0_ffn_expert_up, np.float32)], axis=1))
    zeros_pre = np.zeros((2048, 1024), np.float32)

    in_maps = []
    for c in range(8):
        b, h2 = c // 2, c % 2
        x_ownc = np.ascontiguousarray(x[b, h2 * 2048:(h2 + 1) * 2048, :])
        x_prec = np.ascontiguousarray(x[b, 0:2048, :]) if h2 == 1 else zeros_pre
        in_maps.append(dict(x_own=x_ownc, x_pre=x_prec, wpT=wpT, wmT=wmT,
                            wqT=wqT, keysT=keysT, pmT=pmT, trf=trfa, nw_in=nw,
                            comb=comb))
    return in_maps


def kernel(**inputs):
    in_maps = make_in_maps(**inputs)
    nc = _get_nc(False)
    res = run_bass_kernel_spmd(nc, in_maps, core_ids=list(range(8)))
    outp = np.empty((4, 4096, 1024), np.float32)
    for c in range(8):
        b, h2 = c // 2, c % 2
        outp[b, h2 * 2048:(h2 + 1) * 2048, :] = res.results[c]["out"]
    return outp
